# version 01
# baseline (speedup 1.0000x reference)
import sys
sys.path.insert(0, "/opt/trn_rl_repo")
from contextlib import ExitStack
import numpy as np
import concourse.bass as bass
import concourse.mybir as mybir
from concourse.bass_utils import run_bass_kernel_spmd

F32 = mybir.dt.float32
BF16 = mybir.dt.bfloat16
AF = mybir.ActivationFunctionType
ALU = mybir.AluOpType

D = 1024
NJ = 8
NS = 16
DTR = 64
T = 256
TG = 512
DH = 4096
MH = DH // 128
EPS = 1e-5

WOFF = {}
_off = 0
for _name, _n in (("w1", D * DH), ("w2", D * DH), ("h1", DH * DH),
                  ("h2", DH * DH), ("w3", DH * D)):
    WOFF[_name] = _off
    _off += _n
WTOT = _off
CONV_PIECE = 128 * 256

QKEYS = ("sq", "aq", "gq")
ALLKEYS = ("pe", "ve", "se", "po", "sq", "aq", "gq")


class Res:
    __slots__ = ("w", "r")
    def __init__(self):
        self.w = None
        self.r = []


class Eng:
    SELF_SYNC = ("ve", "se", "po")

    def __init__(self, key, cnt, default_q=None):
        self.key = key
        self.cnt = cnt
        self.ops = []
        self.hi = {}
        self.default_q = default_q
        self.self_sync = key in Eng.SELF_SYNC

    def _wait(self, ev):
        k, v = ev
        if k == self.key and not self.self_sync:
            return
        if self.hi.get(k, 0) >= v:
            return
        self.hi[k] = v
        self.ops.append(("w", k, v))

    def acquire(self, rd=(), wr=()):
        for b in rd:
            if b.w is not None:
                self._wait(b.w)
        for b in wr:
            if b.w is not None:
                self._wait(b.w)
            for ev in b.r:
                self._wait(ev)

    def op(self, fn, rd=(), wr=(), inc=True, q=None):
        self.acquire(rd=rd, wr=wr)
        q = q if q is not None else self.default_q
        key = q if q is not None else self.key
        step = 16 if q is not None else 1
        self.ops.append(("o", fn, key if inc else None))
        if not inc:
            return None
        self.cnt[key] += step
        ev = (key, self.cnt[key])
        for b in rd:
            b.r.append(ev)
        for b in wr:
            b.w = ev
            b.r = []
        return ev

    def wait_cnt(self, key):
        self._wait((key, self.cnt[key]))


def build_program(seq_len, dbg=False):
    NCH = seq_len // T
    NG = NCH // 2
    assert NCH % 2 == 0
    nc = bass.Bass("TRN2", target_bir_lowering=False,
                   detect_race_conditions=False)

    x_h = nc.dram_tensor("x", [seq_len, D], F32, kind="ExternalInput").ap()
    y_h = nc.dram_tensor("y", [seq_len, D], F32, kind="ExternalOutput").ap()
    wflat = nc.dram_tensor("wflat", [WTOT], F32, kind="ExternalInput").ap()
    xpw_h = nc.dram_tensor("xpw", [128, NJ * 128], F32, kind="ExternalInput").ap()
    dtw_h = nc.dram_tensor("dtw", [DTR, NJ * 128], F32, kind="ExternalInput").ap()
    alog_h = nc.dram_tensor("alog", [128, NS], F32, kind="ExternalInput").ap()
    dtb_h = nc.dram_tensor("dtb", [128, NJ], F32, kind="ExternalInput").ap()
    dvec_h = nc.dram_tensor("dvec", [128, NJ], F32, kind="ExternalInput").ap()
    s1_h = nc.dram_tensor("s1v", [128, MH], F32, kind="ExternalInput").ap()
    s2_h = nc.dram_tensor("s2v", [128, MH], F32, kind="ExternalInput").ap()
    q1_h = nc.dram_tensor("q1v", [128, MH], F32, kind="ExternalInput").ap()
    q2_h = nc.dram_tensor("q2v", [128, MH], F32, kind="ExternalInput").ap()
    s3_h = nc.dram_tensor("s3v", [128, NJ], F32, kind="ExternalInput").ap()
    eall_h = nc.dram_tensor("eall", [NS, NS * 128], BF16, kind="ExternalInput").ap()
    eye_h = nc.dram_tensor("eye", [128, 128], F32, kind="ExternalInput").ap()
    ones_h = nc.dram_tensor("onescol", [128, 1], F32, kind="ExternalInput").ap()
    wbf = nc.dram_tensor("wbf", [WTOT], BF16, kind="Internal").ap()
    rowd = nc.dram_tensor("rowd", [4, T], F32, kind="Internal").ap()
    dbg_t = {}
    if dbg:
        for nm, shape, dt_ in (("dbg_uln", [128, NJ * T], F32),
                               ("dbg_delta", [128, NJ * T], F32),
                               ("dbg_ball", [128, NS * T], BF16),
                               ("dbg_call", [128, NS * T], BF16),
                               ("dbg_x2c0", [128, NJ * T], F32),
                               ("dbg_x2c1", [128, NJ * T], F32),
                               ("dbg_state", [128, 128], F32),
                               ("dbg_ln2b", [128, NJ * TG], BF16),
                               ("dbg_hbuf", [128, MH * TG], BF16),
                               ("dbg_g1", [128, MH * TG], BF16),
                               ("dbg_wbf", [2 * CONV_PIECE], BF16)):
            dbg_t[nm] = nc.dram_tensor(nm, shape, dt_, kind="ExternalOutput").ap()

    ctx = ExitStack()
    def sbuf(name, shape, dt=F32):
        return ctx.enter_context(nc.sbuf_tensor(name, shape, dt))
    def psum(name, shape, dt=F32):
        return ctx.enter_context(nc.psum_tensor(name, shape, dt))

    with ctx:
        x_tm = [sbuf(f"x_tm{i}", [128, D]) for i in range(2)]
        u_tm = sbuf("u_tm", [128, D])
        x2 = [sbuf(f"x2_{i}", [128, NJ * T]) for i in range(4)]
        u_ln = [sbuf(f"uln{i}", [128, NJ * T]) for i in range(2)]
        ulnb = [sbuf(f"ulnb{i}", [128, T], BF16) for i in range(2)]
        delta = [sbuf(f"delta{i}", [128, NJ * T]) for i in range(2)]
        wtile = [sbuf(f"wt{i}", [128, T]) for i in range(2)]
        dBu = [sbuf(f"dBu{i}", [128, T]) for i in range(2)]
        htile = [sbuf(f"ht{i}", [128, T]) for i in range(2)]
        b_all = sbuf("ball", [128, NS * T], BF16)
        c_all = sbuf("call", [128, NS * T], BF16)
        misc = [sbuf(f"misc{i}", [128, T]) for i in range(2)]
        state = sbuf("state", [128, 128])
        sb6 = sbuf("sb6", [128, 20])
        mv = sbuf("mv", [128, 2])
        rstd1 = sbuf("rstd1", [128, 1])
        mr1 = sbuf("mr1", [128, 1])
        bcmp = [sbuf(f"bcmp{i}", [NS, T], BF16) for i in range(2)]
        ccmp = [sbuf(f"ccmp{i}", [NS, T], BF16) for i in range(2)]
        dtraw = [sbuf(f"dtraw{i}", [DTR, T], BF16) for i in range(2)]
        rows = sbuf("rows", [1, 2 * T])
        rrep = [sbuf(f"rrep{i}", [128, T]) for i in range(2)]
        ln2b = sbuf("ln2b", [128, NJ * TG], BF16)
        hbuf = sbuf("hbuf", [128, MH * TG], BF16)
        g1buf = sbuf("g1buf", [128, MH * TG], BF16)
        sa = [sbuf(f"sa{i}", [128, TG]) for i in range(1)]
        wsm = [sbuf(f"wsm{i}", [128, NJ * 128], BF16) for i in range(3)]
        wbg = [sbuf(f"wbg{i}", [128, (MH // 4) * 128], BF16) for i in range(2)]
        outev = [sbuf(f"outev{i}", [128, 128]) for i in range(2)]
        convf = [sbuf(f"convf{i}", [128, 256]) for i in range(2)]
        convb = [sbuf(f"convb{i}", [128, 256], BF16) for i in range(2)]
        xpw_b = sbuf("xpwb", [128, NJ * 128], BF16)
        dtw_b = sbuf("dtwb", [DTR, NJ * 128], BF16)
        arep = sbuf("arep", [128, NS])
        dtb_s = sbuf("dtbs", [128, NJ])
        dvec_s = sbuf("dvecs", [128, NJ])
        s1_s = sbuf("s1s", [128, MH])
        s2_s = sbuf("s2s", [128, MH])
        q1_s = sbuf("q1s", [128, MH])
        q2_s = sbuf("q2s", [128, MH])
        s3_s = sbuf("s3s", [128, NJ])
        eall_s = sbuf("ealls", [NS, NS * 128], BF16)
        eye_s = sbuf("eyes", [128, 128])
        ones_s = sbuf("oness", [128, 1])
        eps_v = sbuf("epsv", [128, 1])

        pf = [psum(f"pf{i}", [128, TG]) for i in range(4)]
        peb = [psum(f"peb{i}", [128, TG]) for i in range(2)]
        pp = [psum(f"pp{i}", [128, TG]) for i in range(2)]

        sems = {k: ctx.enter_context(nc.semaphore(k)) for k in ALLKEYS}
        CNT = {k: 0 for k in ALLKEYS}
        PE = Eng("pe", CNT)
        VE = Eng("ve", CNT)
        SE = Eng("se", CNT)
        PO = Eng("po", CNT)
        SQ = Eng("sq", CNT, default_q="sq")

        r_xtm = [Res() for _ in range(2)]
        r_utm = Res()
        r_x2 = [Res() for _ in range(4)]
        r_uln = [Res() for _ in range(2)]
        r_ulnb = [Res() for _ in range(2)]
        r_delta = [Res() for _ in range(2)]
        r_wtile = [Res() for _ in range(2)]
        r_dBu = [Res() for _ in range(2)]
        r_h = [Res() for _ in range(2)]
        r_ball = Res()
        r_call = Res()
        r_misc = [Res() for _ in range(2)]
        r_state = Res()
        r_stat = Res()
        r_bcmp = [Res() for _ in range(2)]
        r_ccmp = [Res() for _ in range(2)]
        r_dtraw = [Res() for _ in range(2)]
        r_rows = Res()
        r_rowd = Res()
        r_rrep = [Res() for _ in range(2)]
        r_ln2b = Res()
        r_hbuf = [Res() for _ in range(MH)]
        r_g1 = [Res() for _ in range(MH)]
        r_sa = [Res() for _ in range(1)]
        r_wsm = [Res() for _ in range(3)]
        r_wbg = [Res() for _ in range(2)]
        r_outev = [Res() for _ in range(2)]
        r_convf = [Res() for _ in range(2)]
        r_convb = [Res() for _ in range(2)]
        r_pf = [Res() for _ in range(4)]
        r_peb = [Res() for _ in range(2)]
        r_pp = [Res() for _ in range(2)]
        r_consts = Res()

        for dst, src in ((arep[:], alog_h), (dtb_s[:], dtb_h), (dvec_s[:], dvec_h),
                         (s1_s[:], s1_h), (s2_s[:], s2_h), (q1_s[:], q1_h),
                         (q2_s[:], q2_h), (s3_s[:], s3_h), (eall_s[:], eall_h),
                         (eye_s[:], eye_h), (ones_s[:], ones_h)):
            SQ.op(lambda e, d=dst, s=src: e.dma_start(out=d, in_=s), wr=[r_consts])
        for q in range(4):
            cs = slice(q * 256, (q + 1) * 256)
            SQ.op(lambda e, cs=cs, q=q: e.dma_start(
                out=convf[q % 2][:], in_=xpw_h[:, cs]), wr=[r_convf[q % 2]])
            PO.op(lambda e, cs=cs, q=q: e.tensor_copy(
                xpw_b[:, cs], convf[q % 2][:]),
                rd=[r_convf[q % 2]], wr=[r_consts])
        for q in range(4):
            cs = slice(q * 256, (q + 1) * 256)
            SQ.op(lambda e, cs=cs, q=q: e.dma_start(
                out=convf[q % 2][0:DTR, 0:256], in_=dtw_h[:, cs]), wr=[r_convf[q % 2]])
            PO.op(lambda e, cs=cs, q=q: e.tensor_copy(
                dtw_b[:, cs], convf[q % 2][0:DTR, 0:256]),
                rd=[r_convf[q % 2]], wr=[r_consts])
        SE.op(lambda e: e.activation(arep[:], arep[:], AF.Exp),
              rd=[r_consts], wr=[r_consts])
        VE.op(lambda e: e.tensor_scalar_mul(arep[:], arep[:], -1.0),
              rd=[r_consts], wr=[r_consts])
        VE.op(lambda e: e.memset(state[:], 0.0), wr=[r_state])
        VE.op(lambda e: e.memset(eps_v[:], EPS), wr=[r_consts])

        conv_done = {}
        piece = 0
        for name, nelem in (("w1", D * DH), ("w2", D * DH), ("h1", DH * DH),
                            ("h2", DH * DH), ("w3", DH * D)):
            for i in range(nelem // CONV_PIECE):
                s = piece % 2
                off = WOFF[name] + i * CONV_PIECE
                fin = wflat[off:off + CONV_PIECE].rearrange("(p f) -> p f", p=128)
                bout = wbf[off:off + CONV_PIECE].rearrange("(p f) -> p f", p=128)
                SQ.op(lambda e, d=convf[s][:], so=fin: e.dma_start(out=d, in_=so),
                      wr=[r_convf[s]])
                PO.op(lambda e, s=s: e.tensor_copy(convb[s][:], convf[s][:]),
                      rd=[r_convf[s]], wr=[r_convb[s]])
                PO.op(lambda e, s=s, do=bout: e.dma_start(out=do, in_=convb[s][:]),
                      rd=[r_convb[s]], q="gq")
                piece += 1
            conv_done[name] = ("gq", CNT["gq"])

        def prep_chunk(k):
            p = k % 2
            x2s, rx2 = x2[k % 4], r_x2[k % 4]
            for c in range(2):
                src_ = x_h[k * T + c * 128: k * T + (c + 1) * 128, :]
                SE.op(lambda e, c=c, s=src_: e.dma_start(out=x_tm[c][:], in_=s),
                      rd=[], wr=[r_xtm[c]], q="aq")
            for c in range(2):
                xt = x_tm[c]
                VE.op(lambda e, xt=xt: e.bn_stats(sb6[:, 0:6], xt[:, 0:512]),
                      rd=[r_xtm[c]], wr=[r_stat])
                VE.op(lambda e, xt=xt: e.bn_stats(sb6[:, 6:12], xt[:, 512:1024]),
                      rd=[r_xtm[c]], wr=[r_stat])
                s6v = sb6[:, 0:12].rearrange("p (g k) -> p g k", k=3)
                means_ap = s6v[:, :, 1]
                cv_ap = s6v[:, :, 2]
                VE.op(lambda e, means_ap=means_ap: e.reduce_sum(
                    mv[:, 0:1], means_ap, mybir.AxisListType.X),
                    rd=[r_stat], wr=[r_stat])
                VE.op(lambda e, means_ap=means_ap: e.tensor_mul(
                    sb6[:, 12:16], means_ap, means_ap), rd=[r_stat], wr=[r_stat])
                VE.op(lambda e: e.reduce_sum(
                    sb6[:, 17:18], sb6[:, 12:16], mybir.AxisListType.X),
                    rd=[r_stat], wr=[r_stat])
                VE.op(lambda e, cv_ap=cv_ap: e.reduce_sum(
                    sb6[:, 16:17], cv_ap, mybir.AxisListType.X),
                    rd=[r_stat], wr=[r_stat])
                VE.op(lambda e: e.tensor_scalar_mul(mv[:, 0:1], mv[:, 0:1], 0.25),
                      rd=[r_stat], wr=[r_stat])
                VE.op(lambda e: e.tensor_scalar_mul(sb6[:, 18:19], sb6[:, 16:17],
                                                    1.0 / D),
                      rd=[r_stat], wr=[r_stat])
                VE.op(lambda e: e.scalar_tensor_tensor(
                    mv[:, 1:2], sb6[:, 17:18], 0.25, sb6[:, 18:19],
                    ALU.mult, ALU.add), rd=[r_stat], wr=[r_stat])
                VE.op(lambda e: e.tensor_mul(sb6[:, 19:20], mv[:, 0:1], mv[:, 0:1]),
                      rd=[r_stat], wr=[r_stat])
                VE.op(lambda e: e.tensor_sub(mv[:, 1:2], mv[:, 1:2], sb6[:, 19:20]),
                      rd=[r_stat], wr=[r_stat])
                SE.op(lambda e: e.activation(rstd1[:], mv[:, 1:2], AF.Ln, bias=eps_v[:, 0:1]),
                      rd=[r_stat], wr=[r_stat])
                SE.op(lambda e: e.activation(rstd1[:], rstd1[:], AF.Exp, scale=-0.5),
                      rd=[r_stat], wr=[r_stat])
                VE.op(lambda e: e.scalar_tensor_tensor(mr1[:], mv[:, 0:1], -1.0,
                                                       rstd1[:], ALU.mult, ALU.mult),
                      rd=[r_stat], wr=[r_stat])
                VE.op(lambda e, xt=xt: e.tensor_scalar(
                    u_tm[:], xt[:], rstd1[:, 0:1], mr1[:, 0:1], ALU.mult, ALU.add),
                    rd=[r_stat, r_xtm[c]], wr=[r_utm])
                for j in range(NJ):
                    bank = j % 2
                    PE.acquire(rd=[r_xtm[c], r_utm, r_consts], wr=[r_pp[bank]])
                    PE.op(lambda e, c=c, j=j, bank=bank: e.transpose(
                        pp[bank][:, 0:128], x_tm[c][:, j * 128:(j + 1) * 128],
                        eye_s[:]), inc=False)
                    PE.op(lambda e, c=c, j=j, bank=bank: e.transpose(
                        pp[bank][:, 128:256], u_tm[:, j * 128:(j + 1) * 128],
                        eye_s[:]),
                        rd=[r_xtm[c], r_utm], wr=[r_pp[bank]])
                    SE.op(lambda e, j=j, c=c, bank=bank, x2s=x2s: e.copy(
                        x2s[:, j * T + c * 128: j * T + (c + 1) * 128],
                        pp[bank][:, 0:128]),
                        rd=[r_pp[bank]], wr=[rx2])
                    SE.op(lambda e, j=j, c=c, p=p, bank=bank: e.copy(
                        u_ln[p][:, j * T + c * 128: j * T + (c + 1) * 128],
                        pp[bank][:, 128:256]),
                        rd=[r_pp[bank]], wr=[r_uln[p]])
            for j in range(NJ):
                SE.op(lambda e, j=j, p=p: e.copy(
                    ulnb[j % 2][:], u_ln[p][:, j * T:(j + 1) * T]),
                    rd=[r_uln[p]], wr=[r_ulnb[j % 2]])
                if j == 0:
                    PE.acquire(wr=[r_peb[0]])
                PE.op(lambda e, j=j: e.matmul(
                    peb[0][:, 0:T], xpw_b[:, j * 128:(j + 1) * 128],
                    ulnb[j % 2][:], start=(j == 0), stop=(j == NJ - 1)),
                    rd=[r_ulnb[j % 2], r_consts],
                    wr=[r_peb[0]] if j == NJ - 1 else (), inc=(j == NJ - 1))
            SE.op(lambda e, p=p: e.copy(bcmp[p][:], peb[0][0:NS, 0:T]),
                  rd=[r_peb[0]], wr=[r_bcmp[p]])
            SE.op(lambda e, p=p: e.copy(ccmp[p][:], peb[0][32:48, 0:T]),
                  rd=[r_peb[0]], wr=[r_ccmp[p]])
            SE.op(lambda e, p=p: e.copy(dtraw[p][:], peb[0][64:128, 0:T]),
                  rd=[r_peb[0]], wr=[r_dtraw[p]])
            for j in range(NJ):
                bank = j % 2
                PE.op(lambda e, j=j, p=p, bank=bank: e.matmul(
                    pp[bank][:, 0:T], dtw_b[:, j * 128:(j + 1) * 128], dtraw[p][:],
                    start=True, stop=True),
                    rd=[r_dtraw[p], r_consts], wr=[r_pp[bank]])
                VE.op(lambda e, j=j, bank=bank: e.tensor_scalar(
                    misc[0][:], pp[bank][:, 0:T], dtb_s[:, j:j + 1], None,
                    ALU.add),
                    rd=[r_pp[bank], r_consts], wr=[r_misc[0]])
                SE.op(lambda e: e.activation(misc[1][:], misc[0][:], AF.Relu),
                      rd=[r_misc[0]], wr=[r_misc[1]])
                SE.op(lambda e: e.activation(misc[0][:], misc[0][:], AF.Abs),
                      rd=[r_misc[0]], wr=[r_misc[0]])
                SE.op(lambda e: e.activation(misc[0][:], misc[0][:], AF.Exp, scale=-1.0),
                      rd=[r_misc[0]], wr=[r_misc[0]])
                SE.op(lambda e: e.activation(misc[0][:], misc[0][:], AF.Ln, bias=1.0),
                      rd=[r_misc[0]], wr=[r_misc[0]])
                VE.op(lambda e, j=j, p=p: e.tensor_add(
                    delta[p][:, j * T:(j + 1) * T], misc[1][:], misc[0][:]),
                    rd=[r_misc[1], r_misc[0]], wr=[r_delta[p]])

        def scan_begin(k):
            p = k % 2
            for n in range(NS):
                bank = n % 2
                esl = eall_s[:, n * 128:(n + 1) * 128]
                PE.acquire(rd=[r_bcmp[p], r_ccmp[p], r_consts], wr=[r_peb[bank]])
                PE.op(lambda e, bank=bank, esl=esl, p=p: e.matmul(
                    peb[bank][:, 0:T], esl, bcmp[p][:], start=True, stop=True),
                    inc=False)
                PE.op(lambda e, bank=bank, esl=esl, p=p: e.matmul(
                    peb[bank][:, T:2 * T], esl, ccmp[p][:], start=True, stop=True),
                    rd=[r_bcmp[p], r_ccmp[p]], wr=[r_peb[bank]])
                SE.op(lambda e, n=n, bank=bank: e.copy(
                    b_all[:, n * T:(n + 1) * T], peb[bank][:, 0:T]),
                    rd=[r_peb[bank]], wr=[r_ball])
                SE.op(lambda e, n=n, bank=bank: e.copy(
                    c_all[:, n * T:(n + 1) * T], peb[bank][:, T:2 * T]),
                    rd=[r_peb[bank]], wr=[r_call])

        def scan_visit(k, j):
            p = k % 2
            js = j % 2
            x2s, rx2 = x2[k % 4], r_x2[k % 4]
            jT = slice(j * T, (j + 1) * T)
            VE.op(lambda e, jT=jT, p=p, js=js: e.tensor_mul(
                wtile[js][:], delta[p][:, jT], u_ln[p][:, jT]),
                rd=[r_delta[p], r_uln[p]], wr=[r_wtile[js]])
            VE.op(lambda e, jT=jT, p=p, j=j, x2s=x2s: e.scalar_tensor_tensor(
                x2s[:, jT], u_ln[p][:, jT], dvec_s[:, j:j + 1], x2s[:, jT],
                ALU.mult, ALU.add),
                rd=[r_uln[p], r_consts, rx2], wr=[rx2])
            for sb in range(4):
                for i in range(4):
                    n = sb * 4 + i
                    bank = i // 2
                    dcol = slice((i % 2) * T, (i % 2 + 1) * T)
                    SE.op(lambda e, jT=jT, dcol=dcol, p=p, n=n, bank=bank: e.activation(
                        pp[bank][:, dcol], delta[p][:, jT], AF.Exp,
                        scale=arep[:, n:n + 1]),
                        rd=[r_delta[p], r_consts], wr=[r_pp[bank]])
                for i in range(4):
                    n = sb * 4 + i
                    bank = i // 2
                    dcol = slice((i % 2) * T, (i % 2 + 1) * T)
                    nT = slice(n * T, (n + 1) * T)
                    col = j * NS + n
                    db, rdb = dBu[n % 2], r_dBu[n % 2]
                    hs, rh = htile[n % 2], r_h[n % 2]
                    PO.op(lambda e, nT=nT, js=js, db=db: e.tensor_mul(
                        db[:], wtile[js][:], b_all[:, nT]),
                        rd=[r_wtile[js], r_ball], wr=[rdb])
                    VE.op(lambda e, dcol=dcol, bank=bank, db=db, hs=hs,
                          col=col: e.tensor_tensor_scan(
                        hs[:], pp[bank][:, dcol], db[:], state[:, col:col + 1],
                        ALU.mult, ALU.add),
                        rd=[r_pp[bank], rdb, r_state], wr=[rh])
                    VE.op(lambda e, nT=nT, hs=hs, db=db: e.tensor_mul(
                        db[:], hs[:], c_all[:, nT]),
                        rd=[rh, r_call], wr=[rdb])
                    VE.op(lambda e, jT=jT, db=db, x2s=x2s: e.tensor_add(
                        x2s[:, jT], x2s[:, jT], db[:]),
                        rd=[rdb, rx2], wr=[rx2])
                    SE.op(lambda e, hs=hs, col=col: e.copy(
                        state[:, col:col + 1], hs[:, T - 1:T]),
                        rd=[rh], wr=[r_state])

        def ln2_chunk(k):
            p = k % 2
            x2s, rx2 = x2[k % 4], r_x2[k % 4]
            PE.acquire(wr=[r_peb[0]])
            for j in range(NJ):
                jT = slice(j * T, (j + 1) * T)
                PE.op(lambda e, jT=jT, j=j, x2s=x2s: e.matmul(
                    peb[0][0:1, 0:T], ones_s[:, 0:1], x2s[:, jT],
                    start=(j == 0), stop=(j == NJ - 1)),
                    rd=[rx2, r_consts], wr=[r_peb[0]] if j == NJ - 1 else (),
                    inc=(j == NJ - 1))
            for j in range(NJ):
                jT = slice(j * T, (j + 1) * T)
                sq, rsq = misc[j % 2], r_misc[j % 2]
                SE.op(lambda e, jT=jT, sq=sq, x2s=x2s: e.activation(
                    sq[:], x2s[:, jT], AF.Square), rd=[rx2], wr=[rsq])
                if j == 0:
                    PE.acquire(wr=[r_peb[1]])
                PE.op(lambda e, j=j, sq=sq: e.matmul(
                    peb[1][0:1, 0:T], ones_s[:, 0:1], sq[:],
                    start=(j == 0), stop=(j == NJ - 1)),
                    rd=[rsq, r_consts], wr=[r_peb[1]] if j == NJ - 1 else (),
                    inc=(j == NJ - 1))
            r0 = rows[0:1, 0:T]
            r1_ = rows[0:1, T:2 * T]
            r2 = rrep[0][0:1, 0:T]
            SE.op(lambda e: e.copy(r0, peb[0][0:1, 0:T]), rd=[r_peb[0]], wr=[r_rows])
            SE.op(lambda e: e.copy(r1_, peb[1][0:1, 0:T]), rd=[r_peb[1]], wr=[r_rows])
            VE.op(lambda e: e.tensor_scalar_mul(r0, r0, 1.0 / D),
                  rd=[r_rows], wr=[r_rows])
            VE.op(lambda e: e.tensor_mul(r2, r0, r0),
                  rd=[r_rows], wr=[r_rrep[0]])
            VE.op(lambda e: e.scalar_tensor_tensor(r1_, r1_, 1.0 / D, r2,
                                                   ALU.mult, ALU.subtract),
                  rd=[r_rows, r_rrep[0]], wr=[r_rows])
            SE.op(lambda e: e.activation(r1_, r1_, AF.Ln, bias=eps_v[0:1, 0:1]),
                  rd=[r_rows], wr=[r_rows])
            SE.op(lambda e: e.activation(r1_, r1_, AF.Exp, scale=-0.5),
                  rd=[r_rows], wr=[r_rows])
            VE.op(lambda e: e.scalar_tensor_tensor(r2, r0, -1.0, r1_,
                                                   ALU.mult, ALU.mult),
                  rd=[r_rows], wr=[r_rrep[0]])
            SE.op(lambda e, p=p: e.dma_start(out=rowd[2 * p:2 * p + 1, :], in_=r1_),
                  rd=[r_rows], wr=[r_rowd], q="aq")
            SE.op(lambda e, p=p: e.dma_start(out=rowd[2 * p + 1:2 * p + 2, :], in_=r2),
                  rd=[r_rows, r_rrep[0]], wr=[r_rowd], q="aq")
            SE.op(lambda e, p=p: e.dma_start(
                out=rrep[0][:], in_=rowd[2 * p:2 * p + 1, :].broadcast_to([128, T])),
                rd=[r_rowd], wr=[r_rrep[0]], q="aq")
            SE.op(lambda e, p=p: e.dma_start(
                out=rrep[1][:],
                in_=rowd[2 * p + 1:2 * p + 2, :].broadcast_to([128, T])),
                rd=[r_rowd], wr=[r_rrep[1]], q="aq")
            for j in range(NJ):
                jT = slice(j * T, (j + 1) * T)
                tmp, rtmp = misc[j % 2], r_misc[j % 2]
                VE.op(lambda e, jT=jT, tmp=tmp, x2s=x2s: e.tensor_mul(
                    tmp[:], x2s[:, jT], rrep[0][:]),
                    rd=[rx2, r_rrep[0]], wr=[rtmp])
                dst = ln2b[:, j * TG + p * T: j * TG + (p + 1) * T]
                VE.op(lambda e, tmp=tmp, dst=dst: e.tensor_add(dst, tmp[:], rrep[1][:]),
                      rd=[rtmp, r_rrep[1]], wr=[r_ln2b])

        def w_dma_sm(slot, name, m):
            off = WOFF[name] + m * NJ * 128 * 128
            src_ = wbf[off:off + NJ * 128 * 128].rearrange("(p f) -> p f", p=128)
            SQ._wait(conv_done[name])
            SQ.op(lambda e, d=wsm[slot][:], s=src_: e.dma_start(out=d, in_=s),
                  rd=[], wr=[r_wsm[slot]])

        def w_dma_bg(slot, name, m, quarter):
            n = (MH // 4) * 128 * 128
            off = WOFF[name] + m * MH * 128 * 128 + quarter * n
            src_ = wbf[off:off + n].rearrange("(p f) -> p f", p=128)
            SQ._wait(conv_done[name])
            SQ.op(lambda e, d=wbg[slot][:], s=src_: e.dma_start(out=d, in_=s),
                  rd=[], wr=[r_wbg[slot]])

        def ffn_w1w2(g):
            for m in range(MH):
                s1slot, s2slot = (2 * m) % 3, (2 * m + 1) % 3
                w_dma_sm(s1slot, "w1", m)
                w_dma_sm(s2slot, "w2", m)
                pb1, pb2 = m % 2, 2 + m % 2
                PE.acquire(wr=[r_pf[pb1]])
                for k in range(NJ):
                    last = k == NJ - 1
                    PE.op(lambda e, k=k, pb1=pb1, s1slot=s1slot, last=last: e.matmul(
                        pf[pb1][:], wsm[s1slot][:, k * 128:(k + 1) * 128],
                        ln2b[:, k * TG:(k + 1) * TG], start=(k == 0), stop=last),
                        rd=[r_wsm[s1slot], r_ln2b], wr=[r_pf[pb1]] if last else (),
                        inc=last)
                PE.acquire(wr=[r_pf[pb2]])
                for k in range(NJ):
                    last = k == NJ - 1
                    PE.op(lambda e, k=k, pb2=pb2, s2slot=s2slot, last=last: e.matmul(
                        pf[pb2][:], wsm[s2slot][:, k * 128:(k + 1) * 128],
                        ln2b[:, k * TG:(k + 1) * TG], start=(k == 0), stop=last),
                        rd=[r_wsm[s2slot], r_ln2b], wr=[r_pf[pb2]] if last else (),
                        inc=last)
                SE.op(lambda e, m=m, pb1=pb1: e.activation(
                    sa[0][:], pf[pb1][:], AF.Silu, scale=s1_s[:, m:m + 1]),
                    rd=[r_pf[pb1], r_consts], wr=[r_sa[0]])
                VE.op(lambda e, m=m, pb2=pb2: e.scalar_tensor_tensor(
                    hbuf[:, m * TG:(m + 1) * TG], pf[pb2][:], s2_s[:, m:m + 1],
                    sa[0][:], ALU.mult, ALU.mult),
                    rd=[r_pf[pb2], r_consts, r_sa[0]], wr=[r_hbuf[m]])

        def ffn_h(g, name, src_res, src_buf, dst_res, dst_buf, qscale, visits):
            vi = 0
            for m in range(MH):
                pb = m % 4
                PE.acquire(wr=[r_pf[pb]])
                for quarter in range(4):
                    slot = quarter % 2
                    w_dma_bg(slot, name, m, quarter)
                    for kk in range(MH // 4):
                        k = quarter * (MH // 4) + kk
                        last = k == MH - 1
                        PE.op(lambda e, kk=kk, k=k, pb=pb, slot=slot, last=last,
                              sb_=src_buf: e.matmul(
                            pf[pb][:], wbg[slot][:, kk * 128:(kk + 1) * 128],
                            sb_[:, k * TG:(k + 1) * TG], start=(k == 0), stop=last),
                            rd=[r_wbg[slot], src_res[k]],
                            wr=[r_pf[pb]] if last else (), inc=last)
                SE.op(lambda e, m=m, pb=pb, db=dst_buf, qs=qscale: e.activation(
                    db[:, m * TG:(m + 1) * TG], pf[pb][:], AF.Gelu,
                    scale=qs[:, m:m + 1]),
                    rd=[r_pf[pb], r_consts], wr=[dst_res[m]])
                if m % 4 == 3 and vi < len(visits):
                    visits[vi]()
                    vi += 1
            while vi < len(visits):
                visits[vi]()
                vi += 1

        def ffn_w3(g):
            for m in range(NJ):
                pb = m % 2
                PE.acquire(wr=[r_pf[pb]])
                for quarter in range(4):
                    slot = quarter % 2
                    w_dma_bg(slot, "w3", m, quarter)
                    for kk in range(MH // 4):
                        k = quarter * (MH // 4) + kk
                        last = k == MH - 1
                        PE.op(lambda e, kk=kk, k=k, pb=pb, slot=slot,
                              last=last: e.matmul(
                            pf[pb][:], wbg[slot][:, kk * 128:(kk + 1) * 128],
                            hbuf[:, k * TG:(k + 1) * TG], start=(k == 0), stop=last),
                            rd=[r_wbg[slot], r_hbuf[k]],
                            wr=[r_pf[pb]] if last else (), inc=last)
                for c in range(2):
                    kc = 2 * g + c
                    x2s, rx2 = x2[kc % 4], r_x2[kc % 4]
                    VE.op(lambda e, m=m, c=c, pb=pb, x2s=x2s: e.scalar_tensor_tensor(
                        x2s[:, m * T:(m + 1) * T], pf[pb][:, c * T:(c + 1) * T],
                        s3_s[:, m:m + 1], x2s[:, m * T:(m + 1) * T],
                        ALU.mult, ALU.add),
                        rd=[r_pf[pb], r_consts, rx2], wr=[rx2])
                for cb in range(4):
                    c, ci = cb // 2, cb % 2
                    kc = 2 * g + c
                    x2s, rx2 = x2[kc % 4], r_x2[kc % 4]
                    bank = cb % 2
                    PE.acquire(rd=[rx2, r_consts], wr=[r_pp[bank]])
                    PE.op(lambda e, m=m, ci=ci, bank=bank, x2s=x2s: e.transpose(
                        pp[bank][:, 0:128],
                        x2s[:, m * T + ci * 128: m * T + (ci + 1) * 128], eye_s[:]),
                        rd=[rx2, r_consts], wr=[r_pp[bank]])
                    SE.op(lambda e, cb=cb, bank=bank: e.copy(
                        outev[cb % 2][:], pp[bank][:, 0:128]),
                        rd=[r_pp[bank]], wr=[r_outev[cb % 2]])
                    dst = y_h[g * TG + cb * 128: g * TG + (cb + 1) * 128,
                              m * 128:(m + 1) * 128]
                    SE.op(lambda e, d=dst, cb=cb: e.dma_start(
                        out=d, in_=outev[cb % 2][:]),
                        rd=[r_outev[cb % 2]], q="aq")

        def full_scan(k):
            scan_begin(k)
            for j in range(NJ):
                scan_visit(k, j)
            ln2_chunk(k)

        prep_chunk(0)
        if dbg:
            SE.op(lambda e: e.dma_start(out=dbg_t["dbg_uln"], in_=u_ln[0][:]),
                  rd=[r_uln[0]], q="aq")
            SE.op(lambda e: e.dma_start(out=dbg_t["dbg_delta"], in_=delta[0][:]),
                  rd=[r_delta[0]], q="aq")
        full_scan(0)
        if dbg:
            SE.op(lambda e: e.dma_start(out=dbg_t["dbg_ball"], in_=b_all[:]),
                  rd=[r_ball], q="aq")
            SE.op(lambda e: e.dma_start(out=dbg_t["dbg_call"], in_=c_all[:]),
                  rd=[r_call], q="aq")
            SE.op(lambda e: e.dma_start(out=dbg_t["dbg_x2c0"], in_=x2[0][:]),
                  rd=[r_x2[0]], q="aq")
            SE.op(lambda e: e.dma_start(out=dbg_t["dbg_state"], in_=state[:]),
                  rd=[r_state], q="aq")
        full_scan_1_marker = True
        prep_chunk(1)
        full_scan(1)
        if dbg:
            SE.op(lambda e: e.dma_start(out=dbg_t["dbg_x2c1"], in_=x2[1][:]),
                  rd=[r_x2[1]], q="aq")
            SE.op(lambda e: e.dma_start(out=dbg_t["dbg_ln2b"], in_=ln2b[:]),
                  rd=[r_ln2b], q="aq")
            SE.op(lambda e: e.dma_start(out=dbg_t["dbg_wbf"],
                                        in_=wbf[0:2 * CONV_PIECE]),
                  q="aq")

        for g in range(NG):
            a, b = 2 * g + 2, 2 * g + 3
            ffn_w1w2(g)
            if a < NCH:
                prep_chunk(a)
            if b < NCH:
                prep_chunk(b)
            if a < NCH:
                scan_begin(a)
                visits_a = [(lambda j=j, a=a: scan_visit(a, j)) for j in range(NJ)]
            else:
                visits_a = []
            if dbg and g == 0:
                for km in range(MH):
                    SE.op(lambda e, km=km: e.dma_start(
                        out=dbg_t["dbg_hbuf"][:, km * TG:(km + 1) * TG],
                        in_=hbuf[:, km * TG:(km + 1) * TG]),
                        rd=[r_hbuf[km]], q="aq")
            ffn_h(g, "h1", r_hbuf, hbuf, r_g1, g1buf, q1_s, visits_a)
            if dbg and g == 0:
                for km in range(MH):
                    SE.op(lambda e, km=km: e.dma_start(
                        out=dbg_t["dbg_g1"][:, km * TG:(km + 1) * TG],
                        in_=g1buf[:, km * TG:(km + 1) * TG]),
                        rd=[r_g1[km]], q="aq")
            if a < NCH:
                ln2_chunk(a)
            if b < NCH:
                scan_begin(b)
                visits_b = [(lambda j=j, b=b: scan_visit(b, j)) for j in range(NJ)]
            else:
                visits_b = []
            ffn_h(g, "h2", r_g1, g1buf, r_hbuf, hbuf, q2_s, visits_b)
            if b < NCH:
                ln2_chunk(b)
            ffn_w3(g)

        SE.wait_cnt("aq")
        SQ.wait_cnt("sq")
        PO.wait_cnt("gq")

        engines = {"pe": PE, "ve": VE, "se": SE, "po": PO, "sq": SQ}
        with nc.Block() as block:
            def runner(ename):
                eng = engines[ename]
                def _run(e):
                    for item in eng.ops:
                        if item[0] == "w":
                            e.wait_ge(sems[item[1]], item[2])
                        else:
                            ins = item[1](e)
                            if item[2] is not None:
                                step = 16 if item[2] in QKEYS else 1
                                ins.then_inc(sems[item[2]], step)
                return _run
            block.tensor(runner("pe"))
            block.vector(runner("ve"))
            block.scalar(runner("se"))
            block.gpsimd(runner("po"))
            block.sync(runner("sq"))

    return nc


def _layout_inputs(inputs, seq_len):
    f = np.float32
    import ml_dtypes
    w1 = inputs["L0_ffn_w1_w"]; w2 = inputs["L0_ffn_w2_w"]
    h1 = inputs["L0_ffn_h1_w"]; h2 = inputs["L0_ffn_h2_w"]
    w3 = inputs["L0_ffn_w3_w"]

    def tiles(w, nm, nk):
        return np.ascontiguousarray(
            np.asarray(w, f).reshape(nm, 128, nk, 128).transpose(0, 3, 2, 1))
    wflat = np.concatenate([
        tiles(w1, MH, NJ).reshape(-1), tiles(w2, MH, NJ).reshape(-1),
        tiles(h1, MH, MH).reshape(-1), tiles(h2, MH, MH).reshape(-1),
        tiles(w3, NJ, MH).reshape(-1)])
    assert wflat.size == WTOT

    xpw = np.asarray(inputs["L0_attn_x_proj_w"], f)
    xpw_all = np.zeros((NJ, 128, 128), f)
    for k in range(NJ):
        blk = xpw[:, k * 128:(k + 1) * 128]
        xpw_all[k, :, 0:NS] = blk[DTR:DTR + NS, :].T
        xpw_all[k, :, 32:32 + NS] = blk[DTR + NS:DTR + 2 * NS, :].T
        xpw_all[k, :, 64:128] = blk[0:DTR, :].T
    xpw_host = np.ascontiguousarray(xpw_all.transpose(1, 0, 2).reshape(128, NJ * 128))

    dtw = np.asarray(inputs["L0_attn_dt_proj_w"], f)
    dtw_host = np.ascontiguousarray(dtw.T.reshape(DTR, NJ * 128))

    alog = np.ascontiguousarray(
        np.broadcast_to(np.asarray(inputs["L0_attn_A_log"], f), (128, NS)))
    dtb = np.ascontiguousarray(np.asarray(inputs["L0_attn_dt_proj_b"], f).reshape(NJ, 128).T)
    dvec = np.ascontiguousarray(np.asarray(inputs["L0_attn_D"], f).reshape(NJ, 128).T)
    s1 = np.ascontiguousarray(np.asarray(inputs["L0_ffn_w1_s"], f).reshape(MH, 128).T)
    s2 = np.ascontiguousarray(np.asarray(inputs["L0_ffn_w2_s"], f).reshape(MH, 128).T)
    q1 = np.ascontiguousarray(np.asarray(inputs["L0_ffn_h1_s"], f).reshape(MH, 128).T)
    q2 = np.ascontiguousarray(np.asarray(inputs["L0_ffn_h2_s"], f).reshape(MH, 128).T)
    s3 = np.ascontiguousarray(np.asarray(inputs["L0_ffn_w3_s"], f).reshape(NJ, 128).T)
    eall = np.zeros((NS, NS * 128), ml_dtypes.bfloat16)
    for n in range(NS):
        eall[n, n * 128:(n + 1) * 128] = 1.0
    eye = np.eye(128, dtype=f)
    ones = np.ones((128, 1), f)

    return {
        "wflat": wflat, "xpw": xpw_host, "dtw": dtw_host, "alog": alog,
        "dtb": dtb, "dvec": dvec, "s1v": s1, "s2v": s2, "q1v": q1, "q2v": q2,
        "s3v": s3, "eall": eall, "eye": eye, "onescol": ones,
    }


def kernel(**inputs):
    u = np.asarray(inputs["L0_attn_u"], np.float32)
    batch, seq_len, d = u.shape
    assert d == D
    shared = _layout_inputs(inputs, seq_len)
    nc = build_program(seq_len)
    in_maps = []
    for c in range(batch):
        m = dict(shared)
        m["x"] = np.ascontiguousarray(u[c])
        in_maps.append(m)
    res = run_bass_kernel_spmd(nc, in_maps, core_ids=list(range(batch)))
    out = np.stack([np.asarray(r["y"], np.float32) for r in res.results], axis=0)
    return out


# version 02
# speedup vs baseline: 1.0553x; 1.0553x over previous
import sys
sys.path.insert(0, "/opt/trn_rl_repo")
from contextlib import ExitStack
import numpy as np
import concourse.bass as bass
import concourse.mybir as mybir
from concourse.bass_utils import run_bass_kernel_spmd

F32 = mybir.dt.float32
BF16 = mybir.dt.bfloat16
AF = mybir.ActivationFunctionType
ALU = mybir.AluOpType

D = 1024
NJ = 8
NS = 16
DTR = 64
T = 256
TG = 512
DH = 4096
MH = DH // 128
EPS = 1e-5

WOFF = {}
_off = 0
for _name, _n in (("w1", D * DH), ("w2", D * DH), ("h1", DH * DH),
                  ("h2", DH * DH), ("w3", DH * D)):
    WOFF[_name] = _off
    _off += _n
WTOT = _off
CONV_PIECE = 128 * 256

QKEYS = ("sq", "aq", "gq")
ALLKEYS = ("pe", "ve", "se", "po", "sq", "aq", "gq")


class Res:
    __slots__ = ("w", "r")
    def __init__(self):
        self.w = None
        self.r = []


class Eng:
    SELF_SYNC = ("ve", "se", "po")

    def __init__(self, key, cnt, default_q=None):
        self.key = key
        self.cnt = cnt
        self.ops = []
        self.hi = {}
        self.default_q = default_q
        self.self_sync = key in Eng.SELF_SYNC

    def _wait(self, ev):
        k, v = ev
        if k == self.key and not self.self_sync:
            return
        if self.hi.get(k, 0) >= v:
            return
        self.hi[k] = v
        self.ops.append(("w", k, v))

    def acquire(self, rd=(), wr=()):
        for b in rd:
            if b.w is not None:
                self._wait(b.w)
        for b in wr:
            if b.w is not None:
                self._wait(b.w)
            for ev in b.r:
                self._wait(ev)

    def op(self, fn, rd=(), wr=(), inc=True, q=None):
        self.acquire(rd=rd, wr=wr)
        q = q if q is not None else self.default_q
        key = q if q is not None else self.key
        step = 16 if q is not None else 1
        self.ops.append(("o", fn, key if inc else None))
        if not inc:
            return None
        self.cnt[key] += step
        ev = (key, self.cnt[key])
        for b in rd:
            b.r.append(ev)
        for b in wr:
            b.w = ev
            b.r = []
        return ev

    def wait_cnt(self, key):
        self._wait((key, self.cnt[key]))


def build_program(seq_len, dbg=False):
    NCH = seq_len // T
    NG = NCH // 2
    assert NCH % 2 == 0
    nc = bass.Bass("TRN2", target_bir_lowering=False,
                   detect_race_conditions=False)

    x_h = nc.dram_tensor("x", [seq_len, D], F32, kind="ExternalInput").ap()
    y_h = nc.dram_tensor("y", [seq_len, D], F32, kind="ExternalOutput").ap()
    wflat = nc.dram_tensor("wflat", [WTOT], F32, kind="ExternalInput").ap()
    xpw_h = nc.dram_tensor("xpw", [128, NJ * 128], F32, kind="ExternalInput").ap()
    dtw_h = nc.dram_tensor("dtw", [DTR, NJ * 128], F32, kind="ExternalInput").ap()
    alog_h = nc.dram_tensor("alog", [128, NS], F32, kind="ExternalInput").ap()
    dtb_h = nc.dram_tensor("dtb", [128, NJ], F32, kind="ExternalInput").ap()
    dvec_h = nc.dram_tensor("dvec", [128, NJ], F32, kind="ExternalInput").ap()
    s1_h = nc.dram_tensor("s1v", [128, MH], F32, kind="ExternalInput").ap()
    s2_h = nc.dram_tensor("s2v", [128, MH], F32, kind="ExternalInput").ap()
    q1_h = nc.dram_tensor("q1v", [128, MH], F32, kind="ExternalInput").ap()
    q2_h = nc.dram_tensor("q2v", [128, MH], F32, kind="ExternalInput").ap()
    s3_h = nc.dram_tensor("s3v", [128, NJ], F32, kind="ExternalInput").ap()
    eall_h = nc.dram_tensor("eall", [NS, NS * 128], BF16, kind="ExternalInput").ap()
    eye_h = nc.dram_tensor("eye", [128, 128], F32, kind="ExternalInput").ap()
    ones_h = nc.dram_tensor("onescol", [128, 1], F32, kind="ExternalInput").ap()
    wbf = nc.dram_tensor("wbf", [WTOT], BF16, kind="Internal").ap()
    rowd = nc.dram_tensor("rowd", [4, T], F32, kind="Internal").ap()
    dbg_t = {}
    if dbg:
        for nm, shape, dt_ in (("dbg_uln", [128, NJ * T], F32),
                               ("dbg_delta", [128, NJ * T], F32),
                               ("dbg_ball", [128, NS * T], BF16),
                               ("dbg_call", [128, NS * T], BF16),
                               ("dbg_x2c0", [128, NJ * T], F32),
                               ("dbg_x2c1", [128, NJ * T], F32),
                               ("dbg_state", [128, 128], F32),
                               ("dbg_ln2b", [128, NJ * TG], BF16),
                               ("dbg_hbuf", [128, MH * TG], BF16),
                               ("dbg_g1", [128, MH * TG], BF16),
                               ("dbg_wbf", [2 * CONV_PIECE], BF16)):
            dbg_t[nm] = nc.dram_tensor(nm, shape, dt_, kind="ExternalOutput").ap()

    ctx = ExitStack()
    def sbuf(name, shape, dt=F32):
        return ctx.enter_context(nc.sbuf_tensor(name, shape, dt))
    def psum(name, shape, dt=F32):
        return ctx.enter_context(nc.psum_tensor(name, shape, dt))

    with ctx:
        x_tm = [sbuf(f"x_tm{i}", [128, D]) for i in range(2)]
        u_tm = sbuf("u_tm", [128, D])
        x2 = [sbuf(f"x2_{i}", [128, NJ * T]) for i in range(4)]
        u_ln = [sbuf(f"uln{i}", [128, NJ * T]) for i in range(2)]
        ulnb = [sbuf(f"ulnb{i}", [128, T], BF16) for i in range(2)]
        delta = [sbuf(f"delta{i}", [128, NJ * T]) for i in range(2)]
        wtile = [sbuf(f"wt{i}", [128, T]) for i in range(2)]
        dBu = [sbuf(f"dBu{i}", [128, T]) for i in range(2)]
        htile = [sbuf(f"ht{i}", [128, T]) for i in range(2)]
        b_all = sbuf("ball", [128, NS * T], BF16)
        c_all = sbuf("call", [128, NS * T], BF16)
        misc = [sbuf(f"misc{i}", [128, T]) for i in range(2)]
        state = sbuf("state", [128, 128])
        sb6 = sbuf("sb6", [128, 20])
        mv = sbuf("mv", [128, 2])
        rstd1 = sbuf("rstd1", [128, 1])
        mr1 = sbuf("mr1", [128, 1])
        bcmp = [sbuf(f"bcmp{i}", [NS, T], BF16) for i in range(2)]
        ccmp = [sbuf(f"ccmp{i}", [NS, T], BF16) for i in range(2)]
        dtraw = [sbuf(f"dtraw{i}", [DTR, T], BF16) for i in range(2)]
        rows = sbuf("rows", [1, 2 * T])
        rrep = [sbuf(f"rrep{i}", [128, T]) for i in range(2)]
        ln2b = sbuf("ln2b", [128, NJ * TG], BF16)
        hbuf = sbuf("hbuf", [128, MH * TG], BF16)
        g1buf = sbuf("g1buf", [128, MH * TG], BF16)
        sa = [sbuf(f"sa{i}", [128, TG]) for i in range(1)]
        wsm = [sbuf(f"wsm{i}", [128, NJ * 128], BF16) for i in range(3)]
        wbg = [sbuf(f"wbg{i}", [128, (MH // 4) * 128], BF16) for i in range(2)]
        outev = [sbuf(f"outev{i}", [128, 128]) for i in range(2)]
        convf = [sbuf(f"convf{i}", [128, 256]) for i in range(2)]
        convb = [sbuf(f"convb{i}", [128, 256], BF16) for i in range(2)]
        xpw_b = sbuf("xpwb", [128, NJ * 128], BF16)
        dtw_b = sbuf("dtwb", [DTR, NJ * 128], BF16)
        arep = sbuf("arep", [128, NS])
        dtb_s = sbuf("dtbs", [128, NJ])
        dvec_s = sbuf("dvecs", [128, NJ])
        s1_s = sbuf("s1s", [128, MH])
        s2_s = sbuf("s2s", [128, MH])
        q1_s = sbuf("q1s", [128, MH])
        q2_s = sbuf("q2s", [128, MH])
        s3_s = sbuf("s3s", [128, NJ])
        eall_s = sbuf("ealls", [NS, NS * 128], BF16)
        eye_s = sbuf("eyes", [128, 128])
        ones_s = sbuf("oness", [128, 1])
        eps_v = sbuf("epsv", [128, 1])

        pf = [psum(f"pf{i}", [128, TG]) for i in range(4)]
        peb = [psum(f"peb{i}", [128, TG]) for i in range(2)]
        pp = [psum(f"pp{i}", [128, TG]) for i in range(2)]

        sems = {k: ctx.enter_context(nc.semaphore(k)) for k in ALLKEYS}
        CNT = {k: 0 for k in ALLKEYS}
        PE = Eng("pe", CNT)
        VE = Eng("ve", CNT)
        SE = Eng("se", CNT)
        PO = Eng("po", CNT)
        SQ = Eng("sq", CNT, default_q="sq")

        r_xtm = [Res() for _ in range(2)]
        r_utm = Res()
        r_x2 = [Res() for _ in range(4)]
        r_uln = [Res() for _ in range(2)]
        r_ulnb = [Res() for _ in range(2)]
        r_delta = [Res() for _ in range(2)]
        r_wtile = [Res() for _ in range(2)]
        r_dBu = [Res() for _ in range(2)]
        r_h = [Res() for _ in range(2)]
        r_ball = Res()
        r_call = Res()
        r_misc = [Res() for _ in range(2)]
        r_state = Res()
        r_stat = Res()
        r_bcmp = [Res() for _ in range(2)]
        r_ccmp = [Res() for _ in range(2)]
        r_dtraw = [Res() for _ in range(2)]
        r_rows = Res()
        r_rowd = Res()
        r_rrep = [Res() for _ in range(2)]
        r_ln2b = Res()
        r_hbuf = [Res() for _ in range(MH)]
        r_g1 = [Res() for _ in range(MH)]
        r_sa = [Res() for _ in range(1)]
        r_wsm = [Res() for _ in range(3)]
        r_wbg = [Res() for _ in range(2)]
        r_outev = [Res() for _ in range(2)]
        r_convf = [Res() for _ in range(2)]
        r_convb = [Res() for _ in range(2)]
        r_pf = [Res() for _ in range(4)]
        r_peb = [Res() for _ in range(2)]
        r_pp = [Res() for _ in range(2)]
        r_consts = Res()

        for dst, src in ((arep[:], alog_h), (dtb_s[:], dtb_h), (dvec_s[:], dvec_h),
                         (s1_s[:], s1_h), (s2_s[:], s2_h), (q1_s[:], q1_h),
                         (q2_s[:], q2_h), (s3_s[:], s3_h), (eall_s[:], eall_h),
                         (eye_s[:], eye_h), (ones_s[:], ones_h)):
            SQ.op(lambda e, d=dst, s=src: e.dma_start(out=d, in_=s), wr=[r_consts])
        for q in range(4):
            cs = slice(q * 256, (q + 1) * 256)
            SQ.op(lambda e, cs=cs, q=q: e.dma_start(
                out=convf[q % 2][:], in_=xpw_h[:, cs]), wr=[r_convf[q % 2]])
            PO.op(lambda e, cs=cs, q=q: e.tensor_copy(
                xpw_b[:, cs], convf[q % 2][:]),
                rd=[r_convf[q % 2]], wr=[r_consts])
        for q in range(4):
            cs = slice(q * 256, (q + 1) * 256)
            SQ.op(lambda e, cs=cs, q=q: e.dma_start(
                out=convf[q % 2][0:DTR, 0:256], in_=dtw_h[:, cs]), wr=[r_convf[q % 2]])
            PO.op(lambda e, cs=cs, q=q: e.tensor_copy(
                dtw_b[:, cs], convf[q % 2][0:DTR, 0:256]),
                rd=[r_convf[q % 2]], wr=[r_consts])
        SE.op(lambda e: e.activation(arep[:], arep[:], AF.Exp),
              rd=[r_consts], wr=[r_consts])
        VE.op(lambda e: e.tensor_scalar_mul(arep[:], arep[:], -1.0),
              rd=[r_consts], wr=[r_consts])
        VE.op(lambda e: e.memset(state[:], 0.0), wr=[r_state])
        VE.op(lambda e: e.memset(eps_v[:], EPS), wr=[r_consts])

        conv_done = {}
        conv_pieces = []
        for name, nelem in (("w1", D * DH), ("w2", D * DH), ("h1", DH * DH),
                            ("h2", DH * DH), ("w3", DH * D)):
            for i in range(nelem // CONV_PIECE):
                conv_pieces.append((name, WOFF[name] + i * CONV_PIECE))
        conv_state = {"idx": 0}

        def drain_conv(n):
            while n > 0 and conv_state["idx"] < len(conv_pieces):
                name, off = conv_pieces[conv_state["idx"]]
                s = conv_state["idx"] % 2
                fin = wflat[off:off + CONV_PIECE].rearrange("(p f) -> p f", p=128)
                bout = wbf[off:off + CONV_PIECE].rearrange("(p f) -> p f", p=128)
                SQ.op(lambda e, d=convf[s][:], so=fin: e.dma_start(out=d, in_=so),
                      wr=[r_convf[s]])
                PO.op(lambda e, s=s: e.tensor_copy(convb[s][:], convf[s][:]),
                      rd=[r_convf[s]], wr=[r_convb[s]])
                PO.op(lambda e, s=s, do=bout: e.dma_start(out=do, in_=convb[s][:]),
                      rd=[r_convb[s]], q="gq")
                conv_state["idx"] += 1
                if off + CONV_PIECE == WOFF[name] + _conv_stage_size[name]:
                    conv_done[name] = ("gq", CNT["gq"])
                n -= 1

        _conv_stage_size = {"w1": D * DH, "w2": D * DH, "h1": DH * DH,
                            "h2": DH * DH, "w3": DH * D}

        def prep_chunk(k):
            p = k % 2
            x2s, rx2 = x2[k % 4], r_x2[k % 4]
            for c in range(2):
                src_ = x_h[k * T + c * 128: k * T + (c + 1) * 128, :]
                SE.op(lambda e, c=c, s=src_: e.dma_start(out=x_tm[c][:], in_=s),
                      rd=[], wr=[r_xtm[c]], q="aq")
            for c in range(2):
                xt = x_tm[c]
                VE.op(lambda e, xt=xt: e.bn_stats(sb6[:, 0:6], xt[:, 0:512]),
                      rd=[r_xtm[c]], wr=[r_stat])
                VE.op(lambda e, xt=xt: e.bn_stats(sb6[:, 6:12], xt[:, 512:1024]),
                      rd=[r_xtm[c]], wr=[r_stat])
                s6v = sb6[:, 0:12].rearrange("p (g k) -> p g k", k=3)
                means_ap = s6v[:, :, 1]
                cv_ap = s6v[:, :, 2]
                VE.op(lambda e, means_ap=means_ap: e.reduce_sum(
                    mv[:, 0:1], means_ap, mybir.AxisListType.X),
                    rd=[r_stat], wr=[r_stat])
                VE.op(lambda e, means_ap=means_ap: e.tensor_mul(
                    sb6[:, 12:16], means_ap, means_ap), rd=[r_stat], wr=[r_stat])
                VE.op(lambda e: e.reduce_sum(
                    sb6[:, 17:18], sb6[:, 12:16], mybir.AxisListType.X),
                    rd=[r_stat], wr=[r_stat])
                VE.op(lambda e, cv_ap=cv_ap: e.reduce_sum(
                    sb6[:, 16:17], cv_ap, mybir.AxisListType.X),
                    rd=[r_stat], wr=[r_stat])
                VE.op(lambda e: e.tensor_scalar_mul(mv[:, 0:1], mv[:, 0:1], 0.25),
                      rd=[r_stat], wr=[r_stat])
                VE.op(lambda e: e.tensor_scalar_mul(sb6[:, 18:19], sb6[:, 16:17],
                                                    1.0 / D),
                      rd=[r_stat], wr=[r_stat])
                VE.op(lambda e: e.scalar_tensor_tensor(
                    mv[:, 1:2], sb6[:, 17:18], 0.25, sb6[:, 18:19],
                    ALU.mult, ALU.add), rd=[r_stat], wr=[r_stat])
                VE.op(lambda e: e.tensor_mul(sb6[:, 19:20], mv[:, 0:1], mv[:, 0:1]),
                      rd=[r_stat], wr=[r_stat])
                VE.op(lambda e: e.tensor_sub(mv[:, 1:2], mv[:, 1:2], sb6[:, 19:20]),
                      rd=[r_stat], wr=[r_stat])
                SE.op(lambda e: e.activation(rstd1[:], mv[:, 1:2], AF.Ln, bias=eps_v[:, 0:1]),
                      rd=[r_stat], wr=[r_stat])
                SE.op(lambda e: e.activation(rstd1[:], rstd1[:], AF.Exp, scale=-0.5),
                      rd=[r_stat], wr=[r_stat])
                VE.op(lambda e: e.scalar_tensor_tensor(mr1[:], mv[:, 0:1], -1.0,
                                                       rstd1[:], ALU.mult, ALU.mult),
                      rd=[r_stat], wr=[r_stat])
                VE.op(lambda e, xt=xt: e.tensor_scalar(
                    u_tm[:], xt[:], rstd1[:, 0:1], mr1[:, 0:1], ALU.mult, ALU.add),
                    rd=[r_stat, r_xtm[c]], wr=[r_utm])
                for j in range(NJ):
                    bank = j % 2
                    PE.acquire(rd=[r_xtm[c], r_utm, r_consts], wr=[r_pp[bank]])
                    PE.op(lambda e, c=c, j=j, bank=bank: e.transpose(
                        pp[bank][:, 0:128], x_tm[c][:, j * 128:(j + 1) * 128],
                        eye_s[:]), inc=False)
                    PE.op(lambda e, c=c, j=j, bank=bank: e.transpose(
                        pp[bank][:, 128:256], u_tm[:, j * 128:(j + 1) * 128],
                        eye_s[:]),
                        rd=[r_xtm[c], r_utm], wr=[r_pp[bank]])
                    SE.op(lambda e, j=j, c=c, bank=bank, x2s=x2s: e.copy(
                        x2s[:, j * T + c * 128: j * T + (c + 1) * 128],
                        pp[bank][:, 0:128]),
                        rd=[r_pp[bank]], wr=[rx2])
                    SE.op(lambda e, j=j, c=c, p=p, bank=bank: e.copy(
                        u_ln[p][:, j * T + c * 128: j * T + (c + 1) * 128],
                        pp[bank][:, 128:256]),
                        rd=[r_pp[bank]], wr=[r_uln[p]])
            for j in range(NJ):
                SE.op(lambda e, j=j, p=p: e.copy(
                    ulnb[j % 2][:], u_ln[p][:, j * T:(j + 1) * T]),
                    rd=[r_uln[p]], wr=[r_ulnb[j % 2]])
                if j == 0:
                    PE.acquire(wr=[r_peb[0]])
                PE.op(lambda e, j=j: e.matmul(
                    peb[0][:, 0:T], xpw_b[:, j * 128:(j + 1) * 128],
                    ulnb[j % 2][:], start=(j == 0), stop=(j == NJ - 1)),
                    rd=[r_ulnb[j % 2], r_consts],
                    wr=[r_peb[0]] if j == NJ - 1 else (), inc=(j == NJ - 1))
            SE.op(lambda e, p=p: e.copy(bcmp[p][:], peb[0][0:NS, 0:T]),
                  rd=[r_peb[0]], wr=[r_bcmp[p]])
            SE.op(lambda e, p=p: e.copy(ccmp[p][:], peb[0][32:48, 0:T]),
                  rd=[r_peb[0]], wr=[r_ccmp[p]])
            SE.op(lambda e, p=p: e.copy(dtraw[p][:], peb[0][64:128, 0:T]),
                  rd=[r_peb[0]], wr=[r_dtraw[p]])
            for j in range(NJ):
                bank = j % 2
                PE.op(lambda e, j=j, p=p, bank=bank: e.matmul(
                    pp[bank][:, 0:T], dtw_b[:, j * 128:(j + 1) * 128], dtraw[p][:],
                    start=True, stop=True),
                    rd=[r_dtraw[p], r_consts], wr=[r_pp[bank]])
                VE.op(lambda e, j=j, bank=bank: e.tensor_scalar(
                    misc[0][:], pp[bank][:, 0:T], dtb_s[:, j:j + 1], None,
                    ALU.add),
                    rd=[r_pp[bank], r_consts], wr=[r_misc[0]])
                SE.op(lambda e: e.activation(misc[1][:], misc[0][:], AF.Relu),
                      rd=[r_misc[0]], wr=[r_misc[1]])
                SE.op(lambda e: e.activation(misc[0][:], misc[0][:], AF.Abs),
                      rd=[r_misc[0]], wr=[r_misc[0]])
                SE.op(lambda e: e.activation(misc[0][:], misc[0][:], AF.Exp, scale=-1.0),
                      rd=[r_misc[0]], wr=[r_misc[0]])
                SE.op(lambda e: e.activation(misc[0][:], misc[0][:], AF.Ln, bias=1.0),
                      rd=[r_misc[0]], wr=[r_misc[0]])
                VE.op(lambda e, j=j, p=p: e.tensor_add(
                    delta[p][:, j * T:(j + 1) * T], misc[1][:], misc[0][:]),
                    rd=[r_misc[1], r_misc[0]], wr=[r_delta[p]])

        def scan_begin(k):
            p = k % 2
            for n in range(NS):
                bank = n % 2
                esl = eall_s[:, n * 128:(n + 1) * 128]
                PE.acquire(rd=[r_bcmp[p], r_ccmp[p], r_consts], wr=[r_peb[bank]])
                PE.op(lambda e, bank=bank, esl=esl, p=p: e.matmul(
                    peb[bank][:, 0:T], esl, bcmp[p][:], start=True, stop=True),
                    inc=False)
                PE.op(lambda e, bank=bank, esl=esl, p=p: e.matmul(
                    peb[bank][:, T:2 * T], esl, ccmp[p][:], start=True, stop=True),
                    rd=[r_bcmp[p], r_ccmp[p]], wr=[r_peb[bank]])
                SE.op(lambda e, n=n, bank=bank: e.copy(
                    b_all[:, n * T:(n + 1) * T], peb[bank][:, 0:T]),
                    rd=[r_peb[bank]], wr=[r_ball])
                SE.op(lambda e, n=n, bank=bank: e.copy(
                    c_all[:, n * T:(n + 1) * T], peb[bank][:, T:2 * T]),
                    rd=[r_peb[bank]], wr=[r_call])

        def scan_visit(k, j):
            p = k % 2
            js = j % 2
            x2s, rx2 = x2[k % 4], r_x2[k % 4]
            jT = slice(j * T, (j + 1) * T)
            VE.op(lambda e, jT=jT, p=p, js=js: e.tensor_mul(
                wtile[js][:], delta[p][:, jT], u_ln[p][:, jT]),
                rd=[r_delta[p], r_uln[p]], wr=[r_wtile[js]])
            VE.op(lambda e, jT=jT, p=p, j=j, x2s=x2s: e.scalar_tensor_tensor(
                x2s[:, jT], u_ln[p][:, jT], dvec_s[:, j:j + 1], x2s[:, jT],
                ALU.mult, ALU.add),
                rd=[r_uln[p], r_consts, rx2], wr=[rx2])
            def emit_exp(n):
                bank = (n // 2) % 2
                dcol = slice((n % 2) * T, (n % 2 + 1) * T)
                SE.op(lambda e, jT=jT, dcol=dcol, p=p, n=n, bank=bank: e.activation(
                    pp[bank][:, dcol], delta[p][:, jT], AF.Exp,
                    scale=arep[:, n:n + 1]),
                    rd=[r_delta[p], r_consts], wr=[r_pp[bank]])

            def emit_scan(n):
                bank = (n // 2) % 2
                dcol = slice((n % 2) * T, (n % 2 + 1) * T)
                col = j * NS + n
                db, rdb = dBu[n % 2], r_dBu[n % 2]
                hs, rh = htile[n % 2], r_h[n % 2]
                PO.op(lambda e, nT=slice(n * T, (n + 1) * T), js=js, db=db:
                      e.tensor_mul(db[:], wtile[js][:], b_all[:, nT]),
                      rd=[r_wtile[js], r_ball], wr=[rdb])
                VE.op(lambda e, dcol=dcol, bank=bank, db=db, hs=hs,
                      col=col: e.tensor_tensor_scan(
                    hs[:], pp[bank][:, dcol], db[:], state[:, col:col + 1],
                    ALU.mult, ALU.add),
                    rd=[r_pp[bank], rdb, r_state], wr=[rh])

            def emit_z(n):
                bank = (n // 2) % 2
                dcol = slice((n % 2) * T, (n % 2 + 1) * T)
                hs, rh = htile[n % 2], r_h[n % 2]
                VE.op(lambda e, nT=slice(n * T, (n + 1) * T), hs=hs, bank=bank,
                      dcol=dcol: e.tensor_mul(
                    pp[bank][:, dcol], hs[:], c_all[:, nT]),
                    rd=[rh, r_call], wr=[r_pp[bank]])
                SE.op(lambda e, hs=hs, col=j * NS + n: e.copy(
                    state[:, col:col + 1], hs[:, T - 1:T]),
                    rd=[rh], wr=[r_state])

            def emit_add(n):
                bank = (n // 2) % 2
                dcol = slice((n % 2) * T, (n % 2 + 1) * T)
                VE.op(lambda e, jT=jT, bank=bank, dcol=dcol, x2s=x2s: e.tensor_add(
                    x2s[:, jT], x2s[:, jT], pp[bank][:, dcol]),
                    rd=[r_pp[bank], rx2], wr=[rx2])

            emit_exp(0); emit_exp(1); emit_exp(2); emit_exp(3)
            for n in range(NS):
                emit_scan(n)
                if n >= 1:
                    emit_z(n - 1)
                if n >= 2:
                    emit_add(n - 2)
                    if n + 2 < NS:
                        emit_exp(n + 2)
            emit_z(NS - 1)
            emit_add(NS - 2)
            emit_add(NS - 1)

        def ln2_chunk(k):
            p = k % 2
            x2s, rx2 = x2[k % 4], r_x2[k % 4]
            PE.acquire(wr=[r_peb[0]])
            for j in range(NJ):
                jT = slice(j * T, (j + 1) * T)
                PE.op(lambda e, jT=jT, j=j, x2s=x2s: e.matmul(
                    peb[0][0:1, 0:T], ones_s[:, 0:1], x2s[:, jT],
                    start=(j == 0), stop=(j == NJ - 1)),
                    rd=[rx2, r_consts], wr=[r_peb[0]] if j == NJ - 1 else (),
                    inc=(j == NJ - 1))
            for j in range(NJ):
                jT = slice(j * T, (j + 1) * T)
                sq, rsq = misc[j % 2], r_misc[j % 2]
                SE.op(lambda e, jT=jT, sq=sq, x2s=x2s: e.activation(
                    sq[:], x2s[:, jT], AF.Square), rd=[rx2], wr=[rsq])
                if j == 0:
                    PE.acquire(wr=[r_peb[1]])
                PE.op(lambda e, j=j, sq=sq: e.matmul(
                    peb[1][0:1, 0:T], ones_s[:, 0:1], sq[:],
                    start=(j == 0), stop=(j == NJ - 1)),
                    rd=[rsq, r_consts], wr=[r_peb[1]] if j == NJ - 1 else (),
                    inc=(j == NJ - 1))
            r0 = rows[0:1, 0:T]
            r1_ = rows[0:1, T:2 * T]
            r2 = rrep[0][0:1, 0:T]
            SE.op(lambda e: e.copy(r0, peb[0][0:1, 0:T]), rd=[r_peb[0]], wr=[r_rows])
            SE.op(lambda e: e.copy(r1_, peb[1][0:1, 0:T]), rd=[r_peb[1]], wr=[r_rows])
            VE.op(lambda e: e.tensor_scalar_mul(r0, r0, 1.0 / D),
                  rd=[r_rows], wr=[r_rows])
            VE.op(lambda e: e.tensor_mul(r2, r0, r0),
                  rd=[r_rows], wr=[r_rrep[0]])
            VE.op(lambda e: e.scalar_tensor_tensor(r1_, r1_, 1.0 / D, r2,
                                                   ALU.mult, ALU.subtract),
                  rd=[r_rows, r_rrep[0]], wr=[r_rows])
            SE.op(lambda e: e.activation(r1_, r1_, AF.Ln, bias=eps_v[0:1, 0:1]),
                  rd=[r_rows], wr=[r_rows])
            SE.op(lambda e: e.activation(r1_, r1_, AF.Exp, scale=-0.5),
                  rd=[r_rows], wr=[r_rows])
            VE.op(lambda e: e.scalar_tensor_tensor(r2, r0, -1.0, r1_,
                                                   ALU.mult, ALU.mult),
                  rd=[r_rows], wr=[r_rrep[0]])
            SE.op(lambda e, p=p: e.dma_start(out=rowd[2 * p:2 * p + 1, :], in_=r1_),
                  rd=[r_rows], wr=[r_rowd], q="aq")
            SE.op(lambda e, p=p: e.dma_start(out=rowd[2 * p + 1:2 * p + 2, :], in_=r2),
                  rd=[r_rows, r_rrep[0]], wr=[r_rowd], q="aq")
            SE.op(lambda e, p=p: e.dma_start(
                out=rrep[0][:], in_=rowd[2 * p:2 * p + 1, :].broadcast_to([128, T])),
                rd=[r_rowd], wr=[r_rrep[0]], q="aq")
            SE.op(lambda e, p=p: e.dma_start(
                out=rrep[1][:],
                in_=rowd[2 * p + 1:2 * p + 2, :].broadcast_to([128, T])),
                rd=[r_rowd], wr=[r_rrep[1]], q="aq")
            for j in range(NJ):
                jT = slice(j * T, (j + 1) * T)
                tmp, rtmp = misc[j % 2], r_misc[j % 2]
                VE.op(lambda e, jT=jT, tmp=tmp, x2s=x2s: e.tensor_mul(
                    tmp[:], x2s[:, jT], rrep[0][:]),
                    rd=[rx2, r_rrep[0]], wr=[rtmp])
                dst = ln2b[:, j * TG + p * T: j * TG + (p + 1) * T]
                VE.op(lambda e, tmp=tmp, dst=dst: e.tensor_add(dst, tmp[:], rrep[1][:]),
                      rd=[rtmp, r_rrep[1]], wr=[r_ln2b])

        def w_dma_sm(slot, name, m):
            off = WOFF[name] + m * NJ * 128 * 128
            src_ = wbf[off:off + NJ * 128 * 128].rearrange("(p f) -> p f", p=128)
            SQ._wait(conv_done[name])
            SQ.op(lambda e, d=wsm[slot][:], s=src_: e.dma_start(out=d, in_=s),
                  rd=[], wr=[r_wsm[slot]])

        bg_slots = [wbg[0], wbg[1], wsm[0], wsm[1], wsm[2]]
        r_bg_slots = [r_wbg[0], r_wbg[1], r_wsm[0], r_wsm[1], r_wsm[2]]

        def w_dma_bg(slot, name, m, quarter):
            n = (MH // 4) * 128 * 128
            off = WOFF[name] + m * MH * 128 * 128 + quarter * n
            src_ = wbf[off:off + n].rearrange("(p f) -> p f", p=128)
            SQ._wait(conv_done[name])
            SQ.op(lambda e, d=bg_slots[slot][:, 0:1024], s=src_: e.dma_start(
                out=d, in_=s), rd=[], wr=[r_bg_slots[slot]])

        def ffn_w1w2(g):
            for m in range(MH):
                drain_conv(8)
                s1slot, s2slot = (2 * m) % 3, (2 * m + 1) % 3
                w_dma_sm(s1slot, "w1", m)
                w_dma_sm(s2slot, "w2", m)
                pb1, pb2 = m % 2, 2 + m % 2
                PE.acquire(wr=[r_pf[pb1]])
                for k in range(NJ):
                    last = k == NJ - 1
                    PE.op(lambda e, k=k, pb1=pb1, s1slot=s1slot, last=last: e.matmul(
                        pf[pb1][:], wsm[s1slot][:, k * 128:(k + 1) * 128],
                        ln2b[:, k * TG:(k + 1) * TG], start=(k == 0), stop=last),
                        rd=[r_wsm[s1slot], r_ln2b], wr=[r_pf[pb1]] if last else (),
                        inc=last)
                PE.acquire(wr=[r_pf[pb2]])
                for k in range(NJ):
                    last = k == NJ - 1
                    PE.op(lambda e, k=k, pb2=pb2, s2slot=s2slot, last=last: e.matmul(
                        pf[pb2][:], wsm[s2slot][:, k * 128:(k + 1) * 128],
                        ln2b[:, k * TG:(k + 1) * TG], start=(k == 0), stop=last),
                        rd=[r_wsm[s2slot], r_ln2b], wr=[r_pf[pb2]] if last else (),
                        inc=last)
                SE.op(lambda e, m=m, pb1=pb1: e.activation(
                    sa[0][:], pf[pb1][:], AF.Silu, scale=s1_s[:, m:m + 1]),
                    rd=[r_pf[pb1], r_consts], wr=[r_sa[0]])
                VE.op(lambda e, m=m, pb2=pb2: e.scalar_tensor_tensor(
                    hbuf[:, m * TG:(m + 1) * TG], pf[pb2][:], s2_s[:, m:m + 1],
                    sa[0][:], ALU.mult, ALU.mult),
                    rd=[r_pf[pb2], r_consts, r_sa[0]], wr=[r_hbuf[m]])

        def ffn_h(g, name, src_res, src_buf, dst_res, dst_buf, qscale, visits):
            vi = 0
            for m in range(MH):
                drain_conv(4)
                pb = m % 4
                PE.acquire(wr=[r_pf[pb]])
                for quarter in range(4):
                    slot = (m * 4 + quarter) % 5
                    w_dma_bg(slot, name, m, quarter)
                    for kk in range(MH // 4):
                        k = quarter * (MH // 4) + kk
                        last = k == MH - 1
                        PE.op(lambda e, kk=kk, k=k, pb=pb, slot=slot, last=last,
                              sb_=src_buf: e.matmul(
                            pf[pb][:], bg_slots[slot][:, kk * 128:(kk + 1) * 128],
                            sb_[:, k * TG:(k + 1) * TG], start=(k == 0), stop=last),
                            rd=[r_bg_slots[slot], src_res[k]],
                            wr=[r_pf[pb]] if last else (), inc=last)
                SE.op(lambda e, m=m, pb=pb, db=dst_buf, qs=qscale: e.activation(
                    db[:, m * TG:(m + 1) * TG], pf[pb][:], AF.Gelu,
                    scale=qs[:, m:m + 1]),
                    rd=[r_pf[pb], r_consts], wr=[dst_res[m]])
                if m % 4 == 3 and vi < len(visits):
                    visits[vi]()
                    vi += 1
            while vi < len(visits):
                visits[vi]()
                vi += 1

        def ffn_w3(g):
            for m in range(NJ):
                pb = m % 2
                PE.acquire(wr=[r_pf[pb]])
                for quarter in range(4):
                    slot = (m * 4 + quarter) % 5
                    w_dma_bg(slot, "w3", m, quarter)
                    for kk in range(MH // 4):
                        k = quarter * (MH // 4) + kk
                        last = k == MH - 1
                        PE.op(lambda e, kk=kk, k=k, pb=pb, slot=slot,
                              last=last: e.matmul(
                            pf[pb][:], bg_slots[slot][:, kk * 128:(kk + 1) * 128],
                            hbuf[:, k * TG:(k + 1) * TG], start=(k == 0), stop=last),
                            rd=[r_bg_slots[slot], r_hbuf[k]],
                            wr=[r_pf[pb]] if last else (), inc=last)
                for c in range(2):
                    kc = 2 * g + c
                    x2s, rx2 = x2[kc % 4], r_x2[kc % 4]
                    VE.op(lambda e, m=m, c=c, pb=pb, x2s=x2s: e.scalar_tensor_tensor(
                        x2s[:, m * T:(m + 1) * T], pf[pb][:, c * T:(c + 1) * T],
                        s3_s[:, m:m + 1], x2s[:, m * T:(m + 1) * T],
                        ALU.mult, ALU.add),
                        rd=[r_pf[pb], r_consts, rx2], wr=[rx2])
                for cb in range(4):
                    c, ci = cb // 2, cb % 2
                    kc = 2 * g + c
                    x2s, rx2 = x2[kc % 4], r_x2[kc % 4]
                    bank = cb % 2
                    PE.acquire(rd=[rx2, r_consts], wr=[r_pp[bank]])
                    PE.op(lambda e, m=m, ci=ci, bank=bank, x2s=x2s: e.transpose(
                        pp[bank][:, 0:128],
                        x2s[:, m * T + ci * 128: m * T + (ci + 1) * 128], eye_s[:]),
                        rd=[rx2, r_consts], wr=[r_pp[bank]])
                    SE.op(lambda e, cb=cb, bank=bank: e.copy(
                        outev[cb % 2][:], pp[bank][:, 0:128]),
                        rd=[r_pp[bank]], wr=[r_outev[cb % 2]])
                    dst = y_h[g * TG + cb * 128: g * TG + (cb + 1) * 128,
                              m * 128:(m + 1) * 128]
                    SE.op(lambda e, d=dst, cb=cb: e.dma_start(
                        out=d, in_=outev[cb % 2][:]),
                        rd=[r_outev[cb % 2]], q="aq")

        def full_scan(k):
            scan_begin(k)
            for j in range(NJ):
                scan_visit(k, j)
                drain_conv(32)
            ln2_chunk(k)

        drain_conv(256)
        prep_chunk(0)
        drain_conv(128)
        if dbg:
            SE.op(lambda e: e.dma_start(out=dbg_t["dbg_uln"], in_=u_ln[0][:]),
                  rd=[r_uln[0]], q="aq")
            SE.op(lambda e: e.dma_start(out=dbg_t["dbg_delta"], in_=delta[0][:]),
                  rd=[r_delta[0]], q="aq")
        full_scan(0)
        if dbg:
            SE.op(lambda e: e.dma_start(out=dbg_t["dbg_ball"], in_=b_all[:]),
                  rd=[r_ball], q="aq")
            SE.op(lambda e: e.dma_start(out=dbg_t["dbg_call"], in_=c_all[:]),
                  rd=[r_call], q="aq")
            SE.op(lambda e: e.dma_start(out=dbg_t["dbg_x2c0"], in_=x2[0][:]),
                  rd=[r_x2[0]], q="aq")
            SE.op(lambda e: e.dma_start(out=dbg_t["dbg_state"], in_=state[:]),
                  rd=[r_state], q="aq")
        full_scan_1_marker = True
        prep_chunk(1)
        full_scan(1)
        if dbg:
            SE.op(lambda e: e.dma_start(out=dbg_t["dbg_x2c1"], in_=x2[1][:]),
                  rd=[r_x2[1]], q="aq")
            SE.op(lambda e: e.dma_start(out=dbg_t["dbg_ln2b"], in_=ln2b[:]),
                  rd=[r_ln2b], q="aq")
            SE.op(lambda e: e.dma_start(out=dbg_t["dbg_wbf"],
                                        in_=wbf[0:2 * CONV_PIECE]),
                  q="aq")

        for g in range(NG):
            a, b = 2 * g + 2, 2 * g + 3
            ffn_w1w2(g)
            if a < NCH:
                prep_chunk(a)
            if b < NCH:
                prep_chunk(b)
            if a < NCH:
                scan_begin(a)
                visits_a = [(lambda j=j, a=a: scan_visit(a, j)) for j in range(NJ)]
            else:
                visits_a = []
            if dbg and g == 0:
                for km in range(MH):
                    SE.op(lambda e, km=km: e.dma_start(
                        out=dbg_t["dbg_hbuf"][:, km * TG:(km + 1) * TG],
                        in_=hbuf[:, km * TG:(km + 1) * TG]),
                        rd=[r_hbuf[km]], q="aq")
            ffn_h(g, "h1", r_hbuf, hbuf, r_g1, g1buf, q1_s, visits_a)
            if dbg and g == 0:
                for km in range(MH):
                    SE.op(lambda e, km=km: e.dma_start(
                        out=dbg_t["dbg_g1"][:, km * TG:(km + 1) * TG],
                        in_=g1buf[:, km * TG:(km + 1) * TG]),
                        rd=[r_g1[km]], q="aq")
            if a < NCH:
                ln2_chunk(a)
            if b < NCH:
                scan_begin(b)
                visits_b = [(lambda j=j, b=b: scan_visit(b, j)) for j in range(NJ)]
            else:
                visits_b = []
            ffn_h(g, "h2", r_g1, g1buf, r_hbuf, hbuf, q2_s, visits_b)
            if b < NCH:
                ln2_chunk(b)
            ffn_w3(g)

        SE.wait_cnt("aq")
        SQ.wait_cnt("sq")
        PO.wait_cnt("gq")

        engines = {"pe": PE, "ve": VE, "se": SE, "po": PO, "sq": SQ}
        with nc.Block() as block:
            def runner(ename):
                eng = engines[ename]
                def _run(e):
                    for item in eng.ops:
                        if item[0] == "w":
                            e.wait_ge(sems[item[1]], item[2])
                        else:
                            ins = item[1](e)
                            if item[2] is not None:
                                step = 16 if item[2] in QKEYS else 1
                                ins.then_inc(sems[item[2]], step)
                return _run
            block.tensor(runner("pe"))
            block.vector(runner("ve"))
            block.scalar(runner("se"))
            block.gpsimd(runner("po"))
            block.sync(runner("sq"))

    return nc


def _layout_inputs(inputs, seq_len):
    f = np.float32
    import ml_dtypes
    w1 = inputs["L0_ffn_w1_w"]; w2 = inputs["L0_ffn_w2_w"]
    h1 = inputs["L0_ffn_h1_w"]; h2 = inputs["L0_ffn_h2_w"]
    w3 = inputs["L0_ffn_w3_w"]

    def tiles(w, nm, nk):
        return np.ascontiguousarray(
            np.asarray(w, f).reshape(nm, 128, nk, 128).transpose(0, 3, 2, 1))
    wflat = np.concatenate([
        tiles(w1, MH, NJ).reshape(-1), tiles(w2, MH, NJ).reshape(-1),
        tiles(h1, MH, MH).reshape(-1), tiles(h2, MH, MH).reshape(-1),
        tiles(w3, NJ, MH).reshape(-1)])
    assert wflat.size == WTOT

    xpw = np.asarray(inputs["L0_attn_x_proj_w"], f)
    xpw_all = np.zeros((NJ, 128, 128), f)
    for k in range(NJ):
        blk = xpw[:, k * 128:(k + 1) * 128]
        xpw_all[k, :, 0:NS] = blk[DTR:DTR + NS, :].T
        xpw_all[k, :, 32:32 + NS] = blk[DTR + NS:DTR + 2 * NS, :].T
        xpw_all[k, :, 64:128] = blk[0:DTR, :].T
    xpw_host = np.ascontiguousarray(xpw_all.transpose(1, 0, 2).reshape(128, NJ * 128))

    dtw = np.asarray(inputs["L0_attn_dt_proj_w"], f)
    dtw_host = np.ascontiguousarray(dtw.T.reshape(DTR, NJ * 128))

    alog = np.ascontiguousarray(
        np.broadcast_to(np.asarray(inputs["L0_attn_A_log"], f), (128, NS)))
    dtb = np.ascontiguousarray(np.asarray(inputs["L0_attn_dt_proj_b"], f).reshape(NJ, 128).T)
    dvec = np.ascontiguousarray(np.asarray(inputs["L0_attn_D"], f).reshape(NJ, 128).T)
    s1 = np.ascontiguousarray(np.asarray(inputs["L0_ffn_w1_s"], f).reshape(MH, 128).T)
    s2 = np.ascontiguousarray(np.asarray(inputs["L0_ffn_w2_s"], f).reshape(MH, 128).T)
    q1 = np.ascontiguousarray(np.asarray(inputs["L0_ffn_h1_s"], f).reshape(MH, 128).T)
    q2 = np.ascontiguousarray(np.asarray(inputs["L0_ffn_h2_s"], f).reshape(MH, 128).T)
    s3 = np.ascontiguousarray(np.asarray(inputs["L0_ffn_w3_s"], f).reshape(NJ, 128).T)
    eall = np.zeros((NS, NS * 128), ml_dtypes.bfloat16)
    for n in range(NS):
        eall[n, n * 128:(n + 1) * 128] = 1.0
    eye = np.eye(128, dtype=f)
    ones = np.ones((128, 1), f)

    return {
        "wflat": wflat, "xpw": xpw_host, "dtw": dtw_host, "alog": alog,
        "dtb": dtb, "dvec": dvec, "s1v": s1, "s2v": s2, "q1v": q1, "q2v": q2,
        "s3v": s3, "eall": eall, "eye": eye, "onescol": ones,
    }


def kernel(**inputs):
    u = np.asarray(inputs["L0_attn_u"], np.float32)
    batch, seq_len, d = u.shape
    assert d == D
    shared = _layout_inputs(inputs, seq_len)
    nc = build_program(seq_len)
    in_maps = []
    for c in range(batch):
        m = dict(shared)
        m["x"] = np.ascontiguousarray(u[c])
        in_maps.append(m)
    res = run_bass_kernel_spmd(nc, in_maps, core_ids=list(range(batch)))
    out = np.stack([np.asarray(r["y"], np.float32) for r in res.results], axis=0)
    return out


# version 10
# speedup vs baseline: 1.2969x; 1.2290x over previous
import sys
sys.path.insert(0, "/opt/trn_rl_repo")
from contextlib import ExitStack
import numpy as np
import concourse.bass as bass
import concourse.mybir as mybir
from concourse.bass_utils import run_bass_kernel_spmd

F32 = mybir.dt.float32
BF16 = mybir.dt.bfloat16
AF = mybir.ActivationFunctionType
ALU = mybir.AluOpType

D = 1024
NJ = 8
NS = 16
DTR = 64
T = 256
TG = 512
DH = 4096
MH = DH // 128
EPS = 1e-5

WOFF = {}
_off = 0
for _name, _n in (("w1", D * DH), ("w2", D * DH), ("h1", DH * DH),
                  ("h2", DH * DH), ("w3", DH * D)):
    WOFF[_name] = _off
    _off += _n
WTOT = _off

TILE = 128 * 1024
BLK = {"w1": 128 * 1024, "w2": 128 * 1024, "h1": 128 * 4096,
       "h2": 128 * 4096, "w3": 128 * 4096}
TILE_TAB = []
for _m in range(MH):
    TILE_TAB.append(("w1", _m, 0))
    TILE_TAB.append(("w2", _m, 0))
for _m in range(MH):
    for _q in range(4):
        TILE_TAB.append(("h1", _m, _q))
for _m in range(MH):
    for _q in range(4):
        TILE_TAB.append(("h2", _m, _q))
for _m in range(NJ):
    for _q in range(4):
        TILE_TAB.append(("w3", _m, _q))
NT = len(TILE_TAB)
assert NT * TILE == WTOT

QKEYS = ("sq", "aq", "gq")
ALLKEYS = ("pe", "ve", "se", "po", "sq", "aq", "gq")


class Res:
    __slots__ = ("w", "r")
    def __init__(self):
        self.w = None
        self.r = []


class Eng:
    SELF_SYNC = ("ve", "se", "po")

    def __init__(self, key, cnt, default_q=None):
        self.key = key
        self.cnt = cnt
        self.ops = []
        self.hi = {}
        self.default_q = default_q
        self.self_sync = key in Eng.SELF_SYNC
        self.pending = []

    def _wait(self, ev):
        k, v = ev
        if k == self.key and not self.self_sync:
            return
        if self.hi.get(k, 0) >= v:
            return
        self.hi[k] = v
        self.ops.append(("w", k, v))

    def acquire(self, rd=(), wr=()):
        for b in rd:
            if b.w is not None:
                self._wait(b.w)
        for b in wr:
            if b.w is not None:
                self._wait(b.w)
            for ev in b.r:
                self._wait(ev)

    def op(self, fn, rd=(), wr=(), inc=True, q=None):
        self.acquire(rd=rd, wr=wr)
        q = q if q is not None else self.default_q
        key = q if q is not None else self.key
        step = 16 if q is not None else 1
        self.ops.append(("o", fn, key if inc else None))
        if not inc:
            assert not wr, "non-inc ops must not declare writes"
            self.pending.extend(rd)
            return None
        self.cnt[key] += step
        ev = (key, self.cnt[key])
        for b in rd:
            b.r.append(ev)
        for b in self.pending:
            b.r.append(ev)
        self.pending = []
        for b in wr:
            b.w = ev
            b.r = []
        return ev

    def wait_cnt(self, key):
        self._wait((key, self.cnt[key]))


def build_program(seq_len):
    NCH = seq_len // T
    NG = NCH // 2
    assert NCH % 2 == 0
    nc = bass.Bass("TRN2", target_bir_lowering=False,
                   detect_race_conditions=False)

    x_h = nc.dram_tensor("x", [seq_len, D], F32, kind="ExternalInput").ap()
    y_h = nc.dram_tensor("y", [seq_len, D], F32, kind="ExternalOutput").ap()
    wflat = nc.dram_tensor("wflat", [WTOT], F32, kind="ExternalInput").ap()
    xpw_h = nc.dram_tensor("xpw", [128, NJ * 128], F32, kind="ExternalInput").ap()
    dtw_h = nc.dram_tensor("dtw", [DTR, NJ * 128], F32, kind="ExternalInput").ap()
    alog_h = nc.dram_tensor("alog", [128, NS], F32, kind="ExternalInput").ap()
    dtb_h = nc.dram_tensor("dtb", [128, NJ], F32, kind="ExternalInput").ap()
    dvec_h = nc.dram_tensor("dvec", [128, NJ], F32, kind="ExternalInput").ap()
    s1_h = nc.dram_tensor("s1v", [128, MH], F32, kind="ExternalInput").ap()
    s2_h = nc.dram_tensor("s2v", [128, MH], F32, kind="ExternalInput").ap()
    q1_h = nc.dram_tensor("q1v", [128, MH], F32, kind="ExternalInput").ap()
    q2_h = nc.dram_tensor("q2v", [128, MH], F32, kind="ExternalInput").ap()
    s3_h = nc.dram_tensor("s3v", [128, NJ], F32, kind="ExternalInput").ap()
    eye_h = nc.dram_tensor("eye", [128, 128], F32, kind="ExternalInput").ap()
    ones_h = nc.dram_tensor("onescol", [128, 1], F32, kind="ExternalInput").ap()
    wbf = nc.dram_tensor("wbf", [WTOT], BF16, kind="Internal").ap()
    rowd = nc.dram_tensor("rowd", [4, T], F32, kind="Internal").ap()
    bcd = nc.dram_tensor("bcd", [4, NS * T], BF16, kind="Internal").ap()

    ctx = ExitStack()
    def sbuf(name, shape, dt=F32):
        return ctx.enter_context(nc.sbuf_tensor(name, shape, dt))
    def psum(name, shape, dt=F32):
        return ctx.enter_context(nc.psum_tensor(name, shape, dt))

    with ctx:
        x_tm = [sbuf(f"x_tm{i}", [128, D]) for i in range(2)]
        u_tm = sbuf("u_tm", [128, D])
        x2 = [sbuf(f"x2_{i}", [128, NJ * T]) for i in range(4)]
        u_ln = [sbuf(f"uln{i}", [128, NJ * T]) for i in range(2)]
        ulnb = [sbuf(f"ulnb{i}", [128, T], BF16) for i in range(2)]
        delta = [sbuf(f"delta{i}", [128, NJ * T]) for i in range(2)]
        wtile = [sbuf(f"wt{i}", [128, T]) for i in range(2)]
        dBu = [sbuf(f"dBu{i}", [128, T]) for i in range(2)]
        htile = [sbuf(f"ht{i}", [128, T]) for i in range(2)]
        b_all = sbuf("ball", [128, NS * T], BF16)
        c_all = sbuf("call", [128, NS * T], BF16)
        misc = [sbuf(f"misc{i}", [128, T]) for i in range(2)]
        state = sbuf("state", [128, 128])
        sb6 = sbuf("sb6", [128, 20])
        mv = sbuf("mv", [128, 2])
        rstd1 = sbuf("rstd1", [128, 1])
        mr1 = sbuf("mr1", [128, 1])
        bcmp = [sbuf(f"bcmp{i}", [NS, T], BF16) for i in range(2)]
        ccmp = [sbuf(f"ccmp{i}", [NS, T], BF16) for i in range(2)]
        dtraw = [sbuf(f"dtraw{i}", [DTR, T], BF16) for i in range(2)]
        rows = sbuf("rows", [1, 2 * T])
        rrep = [sbuf(f"rrep{i}", [128, T]) for i in range(2)]
        ln2b = sbuf("ln2b", [128, NJ * TG], BF16)
        hbuf = sbuf("hbuf", [128, MH * TG], BF16)
        g1buf = sbuf("g1buf", [128, MH * TG], BF16)
        sa = [sbuf(f"sa{i}", [128, TG]) for i in range(1)]
        wslab = sbuf("wslab", [128, 8 * 1024], BF16)
        outev = [sbuf(f"outev{i}", [128, 128]) for i in range(2)]
        xpw_b = sbuf("xpwb", [128, NJ * 128], BF16)
        dtw_b = sbuf("dtwb", [DTR, NJ * 128], BF16)
        arep = sbuf("arep", [128, NS])
        dtb_s = sbuf("dtbs", [128, NJ])
        dvec_s = sbuf("dvecs", [128, NJ])
        s1_s = sbuf("s1s", [128, MH])
        s2_s = sbuf("s2s", [128, MH])
        q1_s = sbuf("q1s", [128, MH])
        q2_s = sbuf("q2s", [128, MH])
        s3_s = sbuf("s3s", [128, NJ])
        eye_s = sbuf("eyes", [128, 128])
        ones_s = sbuf("oness", [128, 1])
        eps_v = sbuf("epsv", [128, 1])

        wsl = [wslab[:, i * 1024:(i + 1) * 1024] for i in range(8)]
        wslab_f = wslab[:].bitcast(F32)
        wsl_f32 = [wslab_f[:, p * 1024:(p + 1) * 1024] for p in range(2)]

        pf = [psum(f"pf{i}", [128, TG]) for i in range(4)]
        pebt = psum("pebt", [128, TG])
        pdt = psum("pdt", [128, TG])
        pp = [psum(f"pp{i}", [128, TG]) for i in range(2)]
        pT = [pebt[:, 0:128], pebt[:, 128:256]]
        pX = pebt[:, 256:512]
        pD = [pdt[:, 0:256], pdt[:, 256:512]]

        sems = {k: ctx.enter_context(nc.semaphore(k)) for k in ALLKEYS}
        CNT = {k: 0 for k in ALLKEYS}
        PE = Eng("pe", CNT)
        VE = Eng("ve", CNT)
        SE = Eng("se", CNT)
        PO = Eng("po", CNT)
        SQ = Eng("sq", CNT, default_q="sq")

        r_xtm = [Res() for _ in range(2)]
        r_utm = Res()
        r_x2 = [Res() for _ in range(4)]
        r_uln = [Res() for _ in range(2)]
        r_ulnb = [Res() for _ in range(2)]
        r_delta = [Res() for _ in range(2)]
        r_wtile = [Res() for _ in range(2)]
        r_dBu = [Res() for _ in range(2)]
        r_h = [Res() for _ in range(2)]
        r_ball = Res()
        r_call = Res()
        r_misc = [Res() for _ in range(2)]
        r_state = Res()
        r_stat = Res()
        r_bcmp = [Res() for _ in range(2)]
        r_ccmp = [Res() for _ in range(2)]
        r_dtraw = [Res() for _ in range(2)]
        r_rows = Res()
        r_rowd = Res()
        r_bcd = [Res() for _ in range(2)]
        r_rrep = [Res() for _ in range(2)]
        r_ln2b = Res()
        r_hbuf = [Res() for _ in range(MH)]
        r_g1 = [Res() for _ in range(MH)]
        r_sa = [Res() for _ in range(1)]
        r_wsl = [Res() for _ in range(8)]
        r_fp = [Res() for _ in range(2)]
        r_outev = [Res() for _ in range(2)]
        r_pf = [Res() for _ in range(4)]
        r_pT = [Res() for _ in range(2)]
        r_pX = Res()
        r_pD = [Res() for _ in range(2)]
        r_pp = [Res() for _ in range(2)]
        r_consts = Res()

        for dst, src in ((arep[:], alog_h), (dtb_s[:], dtb_h), (dvec_s[:], dvec_h),
                         (s1_s[:], s1_h), (s2_s[:], s2_h), (q1_s[:], q1_h),
                         (q2_s[:], q2_h), (s3_s[:], s3_h),
                         (eye_s[:], eye_h), (ones_s[:], ones_h)):
            SQ.op(lambda e, d=dst, s=src: e.dma_start(out=d, in_=s), wr=[r_consts])
        SQ.op(lambda e: e.dma_start(out=x2[2][:, 0:1024], in_=xpw_h), wr=[r_x2[2]])
        SE.op(lambda e: e.copy(xpw_b[:], x2[2][:, 0:1024]),
              rd=[r_x2[2]], wr=[r_consts])
        SQ.op(lambda e: e.dma_start(out=x2[3][0:DTR, 0:1024], in_=dtw_h),
              wr=[r_x2[3]])
        SE.op(lambda e: e.copy(dtw_b[:], x2[3][0:DTR, 0:1024]),
              rd=[r_x2[3]], wr=[r_consts])
        SE.op(lambda e: e.activation(arep[:], arep[:], AF.Exp),
              rd=[r_consts], wr=[r_consts])
        VE.op(lambda e: e.tensor_scalar_mul(arep[:], arep[:], -1.0),
              rd=[r_consts], wr=[r_consts])
        VE.op(lambda e: e.memset(state[:], 0.0), wr=[r_state])
        VE.op(lambda e: e.memset(eps_v[:], EPS), wr=[r_consts])

        def tile_src_f32(i):
            name, m, q = TILE_TAB[i]
            base = WOFF[name] + m * BLK[name]
            ap = wflat[base: base + BLK[name]].rearrange("(p f) -> p f", p=128)
            return ap[:, q * 1024:(q + 1) * 1024]

        def tile_wbf(i):
            return wbf[i * TILE:(i + 1) * TILE].rearrange("(p f) -> p f", p=128)

        store_ev = [None] * NT

        class WStream:
            def __init__(self, g):
                self.g = g
                self.next_p = 0
                self.next_c = 0

            def produce_one(self):
                i = self.next_p
                self.next_p += 1
                if i >= NT:
                    return
                if self.g == 0:
                    p = i % 2
                    oslot = 4 + i % 4
                    src = tile_src_f32(i)
                    SQ.op(lambda e, src=src, p=p: e.dma_start(
                        out=wsl_f32[p], in_=src), wr=[r_fp[p]])
                    PO.op(lambda e, p=p, oslot=oslot: e.tensor_copy(
                        out=wsl[oslot], in_=wsl_f32[p]),
                        rd=[r_fp[p]], wr=[r_wsl[oslot]])
                    store_ev[i] = SE.op(lambda e, i=i, oslot=oslot: e.dma_start(
                        out=tile_wbf(i), in_=wsl[oslot]),
                        rd=[r_wsl[oslot]], q="aq")
                else:
                    slot = i % 8
                    if self.g == 1:
                        SQ._wait(store_ev[i])
                    SQ.op(lambda e, i=i, slot=slot: e.dma_start(
                        out=wsl[slot], in_=tile_wbf(i)), wr=[r_wsl[slot]])

            def consume(self):
                i = self.next_c
                self.next_c += 1
                la = 4 if self.g == 0 else 7
                while self.next_p < min(i + la, NT):
                    self.produce_one()
                slot = (4 + i % 4) if self.g == 0 else i % 8
                return wsl[slot], r_wsl[slot]

        def merge_g0_res():
            for p in range(2):
                for sl in (2 * p, 2 * p + 1):
                    r_wsl[sl].w = r_fp[p].w
                    r_wsl[sl].r = list(r_fp[p].r)

        def prep_chunk(k):
            p = k % 2
            x2s, rx2 = x2[k % 4], r_x2[k % 4]
            for c in range(2):
                src_ = x_h[k * T + c * 128: k * T + (c + 1) * 128, :]
                PO.op(lambda e, c=c, s=src_: e.dma_start(out=x_tm[c][:], in_=s),
                      rd=[], wr=[r_xtm[c]], q="gq")
            for c in range(2):
                xt = x_tm[c]
                VE.op(lambda e, xt=xt: e.bn_stats(sb6[:, 0:6], xt[:, 0:512]),
                      rd=[r_xtm[c]], wr=[r_stat])
                VE.op(lambda e, xt=xt: e.bn_stats(sb6[:, 6:12], xt[:, 512:1024]),
                      rd=[r_xtm[c]], wr=[r_stat])
                s6v = sb6[:, 0:12].rearrange("p (g k) -> p g k", k=3)
                means_ap = s6v[:, :, 1]
                cv_ap = s6v[:, :, 2]
                VE.op(lambda e, means_ap=means_ap: e.reduce_sum(
                    mv[:, 0:1], means_ap, mybir.AxisListType.X),
                    rd=[r_stat], wr=[r_stat])
                VE.op(lambda e, means_ap=means_ap: e.tensor_mul(
                    sb6[:, 12:16], means_ap, means_ap), rd=[r_stat], wr=[r_stat])
                VE.op(lambda e: e.reduce_sum(
                    sb6[:, 17:18], sb6[:, 12:16], mybir.AxisListType.X),
                    rd=[r_stat], wr=[r_stat])
                VE.op(lambda e, cv_ap=cv_ap: e.reduce_sum(
                    sb6[:, 16:17], cv_ap, mybir.AxisListType.X),
                    rd=[r_stat], wr=[r_stat])
                VE.op(lambda e: e.tensor_scalar_mul(mv[:, 0:1], mv[:, 0:1], 0.25),
                      rd=[r_stat], wr=[r_stat])
                VE.op(lambda e: e.tensor_scalar_mul(sb6[:, 18:19], sb6[:, 16:17],
                                                    1.0 / D),
                      rd=[r_stat], wr=[r_stat])
                VE.op(lambda e: e.scalar_tensor_tensor(
                    mv[:, 1:2], sb6[:, 17:18], 0.25, sb6[:, 18:19],
                    ALU.mult, ALU.add), rd=[r_stat], wr=[r_stat])
                VE.op(lambda e: e.tensor_mul(sb6[:, 19:20], mv[:, 0:1], mv[:, 0:1]),
                      rd=[r_stat], wr=[r_stat])
                VE.op(lambda e: e.tensor_sub(mv[:, 1:2], mv[:, 1:2], sb6[:, 19:20]),
                      rd=[r_stat], wr=[r_stat])
                SE.op(lambda e: e.activation(rstd1[:], mv[:, 1:2], AF.Ln, bias=eps_v[:, 0:1]),
                      rd=[r_stat], wr=[r_stat])
                SE.op(lambda e: e.activation(rstd1[:], rstd1[:], AF.Exp, scale=-0.5),
                      rd=[r_stat], wr=[r_stat])
                VE.op(lambda e: e.scalar_tensor_tensor(mr1[:], mv[:, 0:1], -1.0,
                                                       rstd1[:], ALU.mult, ALU.mult),
                      rd=[r_stat], wr=[r_stat])
                VE.op(lambda e, xt=xt: e.tensor_scalar(
                    u_tm[:], xt[:], rstd1[:, 0:1], mr1[:, 0:1], ALU.mult, ALU.add),
                    rd=[r_stat, r_xtm[c]], wr=[r_utm])
                for j in range(NJ):
                    bank = j % 2
                    PE.acquire(rd=[r_xtm[c], r_utm, r_consts], wr=[r_pp[bank]])
                    PE.op(lambda e, c=c, j=j, bank=bank: e.transpose(
                        pp[bank][:, 0:128], x_tm[c][:, j * 128:(j + 1) * 128],
                        eye_s[:]), inc=False)
                    PE.op(lambda e, c=c, j=j, bank=bank: e.transpose(
                        pp[bank][:, 128:256], u_tm[:, j * 128:(j + 1) * 128],
                        eye_s[:]),
                        rd=[r_xtm[c], r_utm], wr=[r_pp[bank]])
                    SE.op(lambda e, j=j, c=c, bank=bank, x2s=x2s: e.copy(
                        x2s[:, j * T + c * 128: j * T + (c + 1) * 128],
                        pp[bank][:, 0:128]),
                        rd=[r_pp[bank]], wr=[rx2])
                    SE.op(lambda e, j=j, c=c, p=p, bank=bank: e.copy(
                        u_ln[p][:, j * T + c * 128: j * T + (c + 1) * 128],
                        pp[bank][:, 128:256]),
                        rd=[r_pp[bank]], wr=[r_uln[p]])
            for j in range(NJ):
                SE.op(lambda e, j=j, p=p: e.copy(
                    ulnb[j % 2][:], u_ln[p][:, j * T:(j + 1) * T]),
                    rd=[r_uln[p]], wr=[r_ulnb[j % 2]])
                if j == 0:
                    PE.acquire(wr=[r_pX])
                PE.op(lambda e, j=j: e.matmul(
                    pX, xpw_b[:, j * 128:(j + 1) * 128],
                    ulnb[j % 2][:], start=(j == 0), stop=(j == NJ - 1)),
                    rd=[r_ulnb[j % 2], r_consts],
                    wr=[r_pX] if j == NJ - 1 else (), inc=True)
            SE.op(lambda e, p=p: e.copy(bcmp[p][:], pebt[0:NS, 256:512]),
                  rd=[r_pX], wr=[r_bcmp[p]])
            SE.op(lambda e, p=p: e.copy(ccmp[p][:], pebt[32:48, 256:512]),
                  rd=[r_pX], wr=[r_ccmp[p]])
            SE.op(lambda e, p=p: e.copy(dtraw[p][:], pebt[64:128, 256:512]),
                  rd=[r_pX], wr=[r_dtraw[p]])
            SE.op(lambda e, p=p: e.dma_start(
                out=bcd[2 * p:2 * p + 1, :].rearrange("o (n t) -> (o n) t", t=T),
                in_=bcmp[p][:]),
                rd=[r_bcmp[p]], wr=[r_bcd[p]], q="aq")
            SE.op(lambda e, p=p: e.dma_start(
                out=bcd[2 * p + 1:2 * p + 2, :].rearrange("o (n t) -> (o n) t", t=T),
                in_=ccmp[p][:]),
                rd=[r_ccmp[p]], wr=[r_bcd[p]], q="aq")
            for j in range(NJ):
                dslot = j % 2
                PE.op(lambda e, j=j, p=p, dslot=dslot: e.matmul(
                    pD[dslot], dtw_b[:, j * 128:(j + 1) * 128], dtraw[p][:],
                    start=True, stop=True),
                    rd=[r_dtraw[p], r_consts], wr=[r_pD[dslot]])
                VE.op(lambda e, j=j, dslot=dslot: e.tensor_scalar(
                    misc[0][:], pD[dslot], dtb_s[:, j:j + 1], None,
                    ALU.add),
                    rd=[r_pD[dslot], r_consts], wr=[r_misc[0]])
                SE.op(lambda e: e.activation(misc[1][:], misc[0][:], AF.Relu),
                      rd=[r_misc[0]], wr=[r_misc[1]])
                SE.op(lambda e: e.activation(misc[0][:], misc[0][:], AF.Abs),
                      rd=[r_misc[0]], wr=[r_misc[0]])
                SE.op(lambda e: e.activation(misc[0][:], misc[0][:], AF.Exp, scale=-1.0),
                      rd=[r_misc[0]], wr=[r_misc[0]])
                SE.op(lambda e: e.activation(misc[0][:], misc[0][:], AF.Ln, bias=1.0),
                      rd=[r_misc[0]], wr=[r_misc[0]])
                VE.op(lambda e, j=j, p=p: e.tensor_add(
                    delta[p][:, j * T:(j + 1) * T], misc[1][:], misc[0][:]),
                    rd=[r_misc[1], r_misc[0]], wr=[r_delta[p]])

        def visit_steps(k, j):
            p = k % 2
            js = j % 2
            x2s, rx2 = x2[k % 4], r_x2[k % 4]
            jT = slice(j * T, (j + 1) * T)

            def head():
                VE.op(lambda e: e.tensor_mul(
                    wtile[js][:], delta[p][:, jT], u_ln[p][:, jT]),
                    rd=[r_delta[p], r_uln[p]], wr=[r_wtile[js]])
                VE.op(lambda e: e.scalar_tensor_tensor(
                    x2s[:, jT], u_ln[p][:, jT], dvec_s[:, j:j + 1], x2s[:, jT],
                    ALU.mult, ALU.add),
                    rd=[r_uln[p], r_consts, rx2], wr=[rx2])

            def emit_exp(n):
                bank = (n // 2) % 2
                dcol = slice((n % 2) * T, (n % 2 + 1) * T)
                SE.op(lambda e, dcol=dcol, n=n, bank=bank: e.activation(
                    pp[bank][:, dcol], delta[p][:, jT], AF.Exp,
                    scale=arep[:, n:n + 1]),
                    rd=[r_delta[p], r_consts], wr=[r_pp[bank]])

            def emit_scan(n):
                bank = (n // 2) % 2
                dcol = slice((n % 2) * T, (n % 2 + 1) * T)
                col = j * NS + n
                db, rdb = dBu[n % 2], r_dBu[n % 2]
                hs, rh = htile[n % 2], r_h[n % 2]
                PO.op(lambda e, nT=slice(n * T, (n + 1) * T), db=db:
                      e.tensor_mul(db[:], wtile[js][:], b_all[:, nT]),
                      rd=[r_wtile[js], r_ball], wr=[rdb])
                VE.op(lambda e, dcol=dcol, bank=bank, db=db, hs=hs,
                      col=col: e.tensor_tensor_scan(
                    hs[:], pp[bank][:, dcol], db[:], state[:, col:col + 1],
                    ALU.mult, ALU.add),
                    rd=[r_pp[bank], rdb, r_state], wr=[rh])

            def emit_z(n):
                bank = (n // 2) % 2
                dcol = slice((n % 2) * T, (n % 2 + 1) * T)
                hs, rh = htile[n % 2], r_h[n % 2]
                VE.op(lambda e, nT=slice(n * T, (n + 1) * T), hs=hs, bank=bank,
                      dcol=dcol: e.tensor_mul(
                    pp[bank][:, dcol], hs[:], c_all[:, nT]),
                    rd=[rh, r_call], wr=[r_pp[bank]])
                PO.op(lambda e, hs=hs, col=j * NS + n: e.tensor_copy(
                    out=state[:, col:col + 1], in_=hs[:, T - 1:T]),
                    rd=[rh], wr=[r_state])

            def emit_add(n):
                bank = (n // 2) % 2
                dcol = slice((n % 2) * T, (n % 2 + 1) * T)
                VE.op(lambda e, bank=bank, dcol=dcol: e.tensor_add(
                    x2s[:, jT], x2s[:, jT], pp[bank][:, dcol]),
                    rd=[r_pp[bank], rx2], wr=[rx2])

            steps = [head, lambda: (emit_exp(0), emit_exp(1)),
                     lambda: (emit_exp(2), emit_exp(3))]
            for n in range(NS):
                def mid(n=n):
                    emit_scan(n)
                    if n >= 1:
                        emit_z(n - 1)
                    if n >= 2:
                        emit_add(n - 2)
                        if n + 2 < NS:
                            emit_exp(n + 2)
                steps.append(mid)
            def tail():
                emit_z(NS - 1)
                emit_add(NS - 2)
                emit_add(NS - 1)
            steps.append(tail)
            return steps

        def chunk_steps(k):
            p = k % 2
            steps = []
            def bc_bcast():
                PO.op(lambda e: e.dma_start(
                    out=b_all[:],
                    in_=bcd[2 * p:2 * p + 1, :].broadcast_to([128, NS * T])),
                    rd=[r_bcd[p]], wr=[r_ball], q="gq")
                PO.op(lambda e: e.dma_start(
                    out=c_all[:],
                    in_=bcd[2 * p + 1:2 * p + 2, :].broadcast_to([128, NS * T])),
                    rd=[r_bcd[p]], wr=[r_call], q="gq")
            steps.append(bc_bcast)
            for j in range(NJ):
                steps.extend(visit_steps(k, j))
            return steps

        def pace(it, n):
            done = 0
            for fn in it:
                fn()
                done += 1
                if done >= n:
                    break

        def ln2_chunk(k):
            p = k % 2
            x2s, rx2 = x2[k % 4], r_x2[k % 4]
            PE.acquire(wr=[r_pD[0], r_pD[1]])
            for j in range(NJ):
                jT = slice(j * T, (j + 1) * T)
                PE.op(lambda e, jT=jT, j=j, x2s=x2s: e.matmul(
                    pdt[0:1, 0:T], ones_s[:, 0:1], x2s[:, jT],
                    start=(j == 0), stop=(j == NJ - 1)),
                    rd=[rx2, r_consts], wr=(), inc=(j == NJ - 1))
            for j in range(NJ):
                jT = slice(j * T, (j + 1) * T)
                sq, rsq = misc[j % 2], r_misc[j % 2]
                SE.op(lambda e, jT=jT, sq=sq, x2s=x2s: e.activation(
                    sq[:], x2s[:, jT], AF.Square), rd=[rx2], wr=[rsq])
                PE.op(lambda e, j=j, sq=sq: e.matmul(
                    pdt[0:1, 256:256 + T], ones_s[:, 0:1], sq[:],
                    start=(j == 0), stop=(j == NJ - 1)),
                    rd=[rsq, r_consts],
                    wr=[r_pD[0], r_pD[1]] if j == NJ - 1 else (),
                    inc=(j == NJ - 1))
            r0 = rows[0:1, 0:T]
            r1_ = rows[0:1, T:2 * T]
            r2 = rrep[0][0:1, 0:T]
            SE.op(lambda e: e.copy(r0, pdt[0:1, 0:T]), rd=[r_pD[0]], wr=[r_rows])
            SE.op(lambda e: e.copy(r1_, pdt[0:1, 256:256 + T]),
                  rd=[r_pD[1]], wr=[r_rows])
            VE.op(lambda e: e.tensor_scalar_mul(r0, r0, 1.0 / D),
                  rd=[r_rows], wr=[r_rows])
            VE.op(lambda e: e.tensor_mul(r2, r0, r0),
                  rd=[r_rows], wr=[r_rrep[0]])
            VE.op(lambda e: e.scalar_tensor_tensor(r1_, r1_, 1.0 / D, r2,
                                                   ALU.mult, ALU.subtract),
                  rd=[r_rows, r_rrep[0]], wr=[r_rows])
            SE.op(lambda e: e.activation(r1_, r1_, AF.Ln, bias=eps_v[0:1, 0:1]),
                  rd=[r_rows], wr=[r_rows])
            SE.op(lambda e: e.activation(r1_, r1_, AF.Exp, scale=-0.5),
                  rd=[r_rows], wr=[r_rows])
            VE.op(lambda e: e.scalar_tensor_tensor(r2, r0, -1.0, r1_,
                                                   ALU.mult, ALU.mult),
                  rd=[r_rows], wr=[r_rrep[0]])
            SE.op(lambda e, p=p: e.dma_start(out=rowd[2 * p:2 * p + 1, :], in_=r1_),
                  rd=[r_rows], wr=[r_rowd], q="aq")
            SE.op(lambda e, p=p: e.dma_start(out=rowd[2 * p + 1:2 * p + 2, :], in_=r2),
                  rd=[r_rows, r_rrep[0]], wr=[r_rowd], q="aq")
            SE.op(lambda e, p=p: e.dma_start(
                out=rrep[0][:], in_=rowd[2 * p:2 * p + 1, :].broadcast_to([128, T])),
                rd=[r_rowd], wr=[r_rrep[0]], q="aq")
            SE.op(lambda e, p=p: e.dma_start(
                out=rrep[1][:],
                in_=rowd[2 * p + 1:2 * p + 2, :].broadcast_to([128, T])),
                rd=[r_rowd], wr=[r_rrep[1]], q="aq")
            for j in range(NJ):
                jT = slice(j * T, (j + 1) * T)
                tmp, rtmp = misc[j % 2], r_misc[j % 2]
                VE.op(lambda e, jT=jT, tmp=tmp, x2s=x2s: e.tensor_mul(
                    tmp[:], x2s[:, jT], rrep[0][:]),
                    rd=[rx2, r_rrep[0]], wr=[rtmp])
                dst = ln2b[:, j * TG + p * T: j * TG + (p + 1) * T]
                VE.op(lambda e, tmp=tmp, dst=dst: e.tensor_add(dst, tmp[:], rrep[1][:]),
                      rd=[rtmp, r_rrep[1]], wr=[r_ln2b])

        def ffn_w1w2(g, st):
            for m in range(MH):
                wv1, rw1 = st.consume()
                pb1, pb2 = m % 2, 2 + m % 2
                PE.acquire(wr=[r_pf[pb1]])
                for k in range(NJ):
                    last = k == NJ - 1
                    PE.op(lambda e, k=k, pb1=pb1, wv1=wv1, last=last: e.matmul(
                        pf[pb1][:], wv1[:, k * 128:(k + 1) * 128],
                        ln2b[:, k * TG:(k + 1) * TG], start=(k == 0), stop=last),
                        rd=[rw1, r_ln2b], wr=[r_pf[pb1]] if last else (),
                        inc=last)
                wv2, rw2 = st.consume()
                PE.acquire(wr=[r_pf[pb2]])
                for k in range(NJ):
                    last = k == NJ - 1
                    PE.op(lambda e, k=k, pb2=pb2, wv2=wv2, last=last: e.matmul(
                        pf[pb2][:], wv2[:, k * 128:(k + 1) * 128],
                        ln2b[:, k * TG:(k + 1) * TG], start=(k == 0), stop=last),
                        rd=[rw2, r_ln2b], wr=[r_pf[pb2]] if last else (),
                        inc=last)
                SE.op(lambda e, m=m, pb1=pb1: e.activation(
                    sa[0][:], pf[pb1][:], AF.Silu, scale=s1_s[:, m:m + 1]),
                    rd=[r_pf[pb1], r_consts], wr=[r_sa[0]])
                VE.op(lambda e, m=m, pb2=pb2: e.scalar_tensor_tensor(
                    hbuf[:, m * TG:(m + 1) * TG], pf[pb2][:], s2_s[:, m:m + 1],
                    sa[0][:], ALU.mult, ALU.mult),
                    rd=[r_pf[pb2], r_consts, r_sa[0]], wr=[r_hbuf[m]])

        def ffn_h(g, st, src_res, src_buf, dst_res, dst_buf, qscale, steps_it):
            for m in range(MH):
                pace(steps_it, 5)
                pb = m % 4
                PE.acquire(wr=[r_pf[pb]])
                for quarter in range(4):
                    wv, rw = st.consume()
                    for kk in range(MH // 4):
                        k = quarter * (MH // 4) + kk
                        last = k == MH - 1
                        PE.op(lambda e, kk=kk, k=k, pb=pb, wv=wv, last=last,
                              sb_=src_buf: e.matmul(
                            pf[pb][:], wv[:, kk * 128:(kk + 1) * 128],
                            sb_[:, k * TG:(k + 1) * TG], start=(k == 0), stop=last),
                            rd=[rw, src_res[k]],
                            wr=[r_pf[pb]] if last else (), inc=last)
                SE.op(lambda e, m=m, pb=pb, db=dst_buf, qs=qscale: e.activation(
                    db[:, m * TG:(m + 1) * TG], pf[pb][:], AF.Gelu,
                    scale=qs[:, m:m + 1]),
                    rd=[r_pf[pb], r_consts], wr=[dst_res[m]])
            pace(steps_it, 10 ** 9)

        def ffn_w3(g, st):
            for m in range(NJ):
                pb = m % 2
                PE.acquire(wr=[r_pf[pb]])
                for quarter in range(4):
                    wv, rw = st.consume()
                    for kk in range(MH // 4):
                        k = quarter * (MH // 4) + kk
                        last = k == MH - 1
                        PE.op(lambda e, kk=kk, k=k, pb=pb, wv=wv,
                              last=last: e.matmul(
                            pf[pb][:], wv[:, kk * 128:(kk + 1) * 128],
                            hbuf[:, k * TG:(k + 1) * TG], start=(k == 0), stop=last),
                            rd=[rw, r_hbuf[k]],
                            wr=[r_pf[pb]] if last else (), inc=last)
                for c in range(2):
                    kc = 2 * g + c
                    x2s, rx2 = x2[kc % 4], r_x2[kc % 4]
                    VE.op(lambda e, m=m, c=c, pb=pb, x2s=x2s: e.scalar_tensor_tensor(
                        x2s[:, m * T:(m + 1) * T], pf[pb][:, c * T:(c + 1) * T],
                        s3_s[:, m:m + 1], x2s[:, m * T:(m + 1) * T],
                        ALU.mult, ALU.add),
                        rd=[r_pf[pb], r_consts, rx2], wr=[rx2])
                for cb in range(4):
                    c, ci = cb // 2, cb % 2
                    kc = 2 * g + c
                    x2s, rx2 = x2[kc % 4], r_x2[kc % 4]
                    bank = cb % 2
                    PE.acquire(rd=[rx2, r_consts], wr=[r_pp[bank]])
                    PE.op(lambda e, m=m, ci=ci, bank=bank, x2s=x2s: e.transpose(
                        pp[bank][:, 0:128],
                        x2s[:, m * T + ci * 128: m * T + (ci + 1) * 128], eye_s[:]),
                        rd=[rx2, r_consts], wr=[r_pp[bank]])
                    SE.op(lambda e, cb=cb, bank=bank: e.copy(
                        outev[cb % 2][:], pp[bank][:, 0:128]),
                        rd=[r_pp[bank]], wr=[r_outev[cb % 2]])
                    dst = y_h[g * TG + cb * 128: g * TG + (cb + 1) * 128,
                              m * 128:(m + 1) * 128]
                    SE.op(lambda e, d=dst, cb=cb: e.dma_start(
                        out=d, in_=outev[cb % 2][:]),
                        rd=[r_outev[cb % 2]], q="aq")

        prep_chunk(0)
        for fn in chunk_steps(0):
            fn()
        ln2_chunk(0)
        prep_chunk(1)
        for fn in chunk_steps(1):
            fn()
        ln2_chunk(1)

        for g in range(NG):
            st = WStream(g)
            a, b = 2 * g + 2, 2 * g + 3
            ffn_w1w2(g, st)
            if a < NCH:
                prep_chunk(a)
            if b < NCH:
                prep_chunk(b)
            it_a = iter(chunk_steps(a)) if a < NCH else iter(())
            ffn_h(g, st, r_hbuf, hbuf, r_g1, g1buf, q1_s, it_a)
            if a < NCH:
                ln2_chunk(a)
            it_b = iter(chunk_steps(b)) if b < NCH else iter(())
            ffn_h(g, st, r_g1, g1buf, r_hbuf, hbuf, q2_s, it_b)
            if b < NCH:
                ln2_chunk(b)
            ffn_w3(g, st)
            assert st.next_c == NT
            if g == 0:
                merge_g0_res()

        SE.wait_cnt("aq")
        SQ.wait_cnt("sq")
        PO.wait_cnt("gq")

        engines = {"pe": PE, "ve": VE, "se": SE, "po": PO, "sq": SQ}
        with nc.Block() as block:
            def runner(ename):
                eng = engines[ename]
                def _run(e):
                    for item in eng.ops:
                        if item[0] == "w":
                            e.wait_ge(sems[item[1]], item[2])
                        else:
                            ins = item[1](e)
                            if item[2] is not None:
                                step = 16 if item[2] in QKEYS else 1
                                ins.then_inc(sems[item[2]], step)
                return _run
            block.tensor(runner("pe"))
            block.vector(runner("ve"))
            block.scalar(runner("se"))
            block.gpsimd(runner("po"))
            block.sync(runner("sq"))

    return nc


def _layout_inputs(inputs, seq_len):
    f = np.float32
    w1 = inputs["L0_ffn_w1_w"]; w2 = inputs["L0_ffn_w2_w"]
    h1 = inputs["L0_ffn_h1_w"]; h2 = inputs["L0_ffn_h2_w"]
    w3 = inputs["L0_ffn_w3_w"]

    def tiles(w, nm, nk):
        return np.ascontiguousarray(
            np.asarray(w, f).reshape(nm, 128, nk, 128).transpose(0, 3, 2, 1))
    wflat = np.concatenate([
        tiles(w1, MH, NJ).reshape(-1), tiles(w2, MH, NJ).reshape(-1),
        tiles(h1, MH, MH).reshape(-1), tiles(h2, MH, MH).reshape(-1),
        tiles(w3, NJ, MH).reshape(-1)])
    assert wflat.size == WTOT

    xpw = np.asarray(inputs["L0_attn_x_proj_w"], f)
    xpw_all = np.zeros((NJ, 128, 128), f)
    for k in range(NJ):
        blk = xpw[:, k * 128:(k + 1) * 128]
        xpw_all[k, :, 0:NS] = blk[DTR:DTR + NS, :].T
        xpw_all[k, :, 32:32 + NS] = blk[DTR + NS:DTR + 2 * NS, :].T
        xpw_all[k, :, 64:128] = blk[0:DTR, :].T
    xpw_host = np.ascontiguousarray(xpw_all.transpose(1, 0, 2).reshape(128, NJ * 128))

    dtw = np.asarray(inputs["L0_attn_dt_proj_w"], f)
    dtw_host = np.ascontiguousarray(dtw.T.reshape(DTR, NJ * 128))

    alog = np.ascontiguousarray(
        np.broadcast_to(np.asarray(inputs["L0_attn_A_log"], f), (128, NS)))
    dtb = np.ascontiguousarray(np.asarray(inputs["L0_attn_dt_proj_b"], f).reshape(NJ, 128).T)
    dvec = np.ascontiguousarray(np.asarray(inputs["L0_attn_D"], f).reshape(NJ, 128).T)
    s1 = np.ascontiguousarray(np.asarray(inputs["L0_ffn_w1_s"], f).reshape(MH, 128).T)
    s2 = np.ascontiguousarray(np.asarray(inputs["L0_ffn_w2_s"], f).reshape(MH, 128).T)
    q1 = np.ascontiguousarray(np.asarray(inputs["L0_ffn_h1_s"], f).reshape(MH, 128).T)
    q2 = np.ascontiguousarray(np.asarray(inputs["L0_ffn_h2_s"], f).reshape(MH, 128).T)
    s3 = np.ascontiguousarray(np.asarray(inputs["L0_ffn_w3_s"], f).reshape(NJ, 128).T)
    eye = np.eye(128, dtype=f)
    ones = np.ones((128, 1), f)

    return {
        "wflat": wflat, "xpw": xpw_host, "dtw": dtw_host, "alog": alog,
        "dtb": dtb, "dvec": dvec, "s1v": s1, "s2v": s2, "q1v": q1, "q2v": q2,
        "s3v": s3, "eye": eye, "onescol": ones,
    }


def kernel(**inputs):
    u = np.asarray(inputs["L0_attn_u"], np.float32)
    batch, seq_len, d = u.shape
    assert d == D
    shared = _layout_inputs(inputs, seq_len)
    nc = build_program(seq_len)
    in_maps = []
    for c in range(batch):
        m = dict(shared)
        m["x"] = np.ascontiguousarray(u[c])
        in_maps.append(m)
    res = run_bass_kernel_spmd(nc, in_maps, core_ids=list(range(batch)))
    out = np.stack([np.asarray(r["y"], np.float32) for r in res.results], axis=0)
    return out
